# version 16
# baseline (speedup 1.0000x reference)
import contextlib
from collections import defaultdict

import numpy as np
import ml_dtypes

import concourse.bass as bass
import concourse.mybir as mybir
from concourse.bass_utils import run_bass_kernel_spmd

F32 = mybir.dt.float32
BF16 = mybir.dt.bfloat16
U16 = mybir.dt.uint16
U32 = mybir.dt.uint32
AF = mybir.ActivationFunctionType
OP = mybir.AluOpType
BF = ml_dtypes.bfloat16



class _Buf:
    __slots__ = ("name", "writers", "readers")

    def __init__(self, name):
        self.name = name
        self.writers = []
        self.readers = []


class _Node:
    __slots__ = ("idx", "engine", "emit", "deps", "dma", "channel",
                 "inc_needed", "sem_after", "tag")

    def __init__(self, idx, engine, emit, dma, channel, tag):
        self.idx = idx
        self.engine = engine
        self.emit = emit
        self.deps = []
        self.dma = dma
        self.channel = channel
        self.inc_needed = dma
        self.sem_after = None
        self.tag = tag


class IR:
    ENGINES = ("tensor", "vector", "scalar", "gpsimd", "sync")

    def __init__(self):
        self.nodes = []
        self._bufs = {}

    def buf(self, name):
        if name not in self._bufs:
            self._bufs[name] = _Buf(name)
        return self._bufs[name]

    def op(self, engine, emit, reads=(), writes=(), writes_accum=(), dma=False,
           tag="", channel=None):
        if dma and channel is None:
            wl = list(writes) + list(writes_accum)
            assert len(wl) == 1, f"dma op must write exactly one buf: {tag}"
            channel = wl[0].name
        n = _Node(len(self.nodes), engine, emit, dma, channel, tag)
        dep_nodes = set()
        for x in reads:
            dep_nodes.update(x.writers)
        for x in writes:
            dep_nodes.update(x.writers)
            dep_nodes.update(x.readers)
        for x in writes_accum:
            dep_nodes.update(x.readers)
        for d in sorted(dep_nodes, key=lambda q: q.idx):
            n.deps.append(d)
            d.inc_needed = True
        self.nodes.append(n)
        for x in reads:
            x.readers.append(n)
        for x in writes:
            x.writers = [n]
            x.readers = []
        for x in writes_accum:
            x.writers.append(n)
        return n

    def barrier(self, tag="bar"):
        live = [x for x in self._bufs.values() if x.writers or x.readers]
        bb = self.buf(f"_{tag}_{len(self.nodes)}")
        gn = self.op("gpsimd", lambda e: e.nop(), reads=live, writes=[bb],
                     tag=tag + "_g")
        ens = []
        for eng in ("tensor", "vector", "scalar", "sync"):
            ens.append(self.op(eng, lambda e: e.nop(), reads=[bb],
                               tag=tag + "_" + eng))
        for x in live:
            x.writers = [gn] + ens
            x.readers = []

    def finalize(self):
        eng_c = defaultdict(int)
        chan_c = defaultdict(int)
        for n in self.nodes:
            if n.dma:
                chan_c[n.channel] += 16
                n.sem_after = chan_c[n.channel]
            elif n.inc_needed:
                eng_c[n.engine] += 1
                n.sem_after = eng_c[n.engine]
        self.per_engine = {e: [n for n in self.nodes if n.engine == e]
                           for e in self.ENGINES}
        self.channels = sorted(chan_c)
        return self

    def sem_key(self, n):
        return ("chan", n.channel) if n.dma else ("eng", n.engine)

    def emit_engine(self, engine, eng, sems):
        waited = defaultdict(int)
        for n in self.per_engine[engine]:
            need = {}
            for d in n.deps:
                k = self.sem_key(d)
                if d.sem_after > need.get(k, 0):
                    need[k] = d.sem_after
            for k, v in sorted(need.items(), key=lambda kv: str(kv[0])):
                if v > waited[k]:
                    eng.wait_ge(sems[k], v)
                    waited[k] = v
            inst = n.emit(eng)
            if n.inc_needed:
                assert inst is not None, f"node {n.idx} ({n.tag}) needs inc, emit=None"
                inst.then_inc(sems[self.sem_key(n)], 16 if n.dma else 1)


def _build_block(nc, ir):
    ir.finalize()
    st = contextlib.ExitStack()
    sems = {}
    for e in IR.ENGINES:
        sems[("eng", e)] = st.enter_context(nc.semaphore(f"se_{e}"))
    for i, ch in enumerate(ir.channels):
        sems[("chan", ch)] = st.enter_context(nc.semaphore(f"sd{i}"))
    block = st.enter_context(nc.Block())

    @block.tensor
    def _(eng):
        ir.emit_engine("tensor", eng, sems)

    @block.vector
    def _(eng):
        ir.emit_engine("vector", eng, sems)

    @block.scalar
    def _(eng):
        ir.emit_engine("scalar", eng, sems)

    @block.gpsimd
    def _(eng):
        ir.emit_engine("gpsimd", eng, sems)

    @block.sync
    def _(eng):
        ir.emit_engine("sync", eng, sems)

    st.close()



DM, DI, NST, RK, KC, NE, HH, DH = 1024, 2048, 16, 64, 4, 8, 64, 256
DMC, DIC = DM // 128, DI // 128
DT_CONST = 0.1
FFN_UPTO = 99
FFN_SK = set()


def build_program(OWN, TF, TM, dbg=False, phases=15):
    NFT, NMC, SUB = OWN // TF, OWN // TM, TF // 128
    nc = bass.Bass("TRN2", target_bir_lowering=False)
    ir = IR()
    gstack = contextlib.ExitStack()
    B = ir.buf
    dram = {}
    sb = {}

    def din(name, shape, dt=F32):
        dram[name] = nc.dram_tensor(name, list(shape), dt, kind="ExternalInput").ap()

    def dout(name, shape, dt=F32):
        dram[name] = nc.dram_tensor(name, list(shape), dt, kind="ExternalOutput").ap()

    def dscr(name, shape, dt=F32):
        if dbg:
            dram[name] = nc.dram_tensor(name, list(shape), dt,
                                        kind="ExternalOutput").ap()
        else:
            dram[name] = nc.dram_tensor(name, list(shape), dt).ap()

    def sbuf(name, shape, dt=F32, stack=None):
        sb[name] = (stack or gstack).enter_context(nc.sbuf_tensor(name, shape, dt))
        return sb[name]

    class MTT:
        def __init__(self, name, nch, width, dt=F32, prow=128, stack=None):
            self.t = sbuf(name, [prow, nch * width], dt, stack)
            self.w = width
            self.name = name

        def __getitem__(self, ch):
            return self.t.ap()[:, ch * self.w:(ch + 1) * self.w]

        def cols(self, ch, c0, c1):
            return self.t.ap()[:, ch * self.w + c0: ch * self.w + c1]

        def ap(self):
            return self.t.ap()

    def ap(n):
        return sb[n].ap()

    din("xloc", [DM, OWN])
    din("selmask", [128, 2])
    din("rw1", [DM, DH]); din("rb1", [128, 2])
    din("rw2", [DH, NE]); din("rb2", [NE, 1])
    din("hw1", [DM, DH]); din("hb1", [128, 2])
    din("hw2", [DH, 1]); din("hb2", [1, 1])
    din("shw1", [2 * DM, HH]); din("shb1", [HH, 1])
    din("shw2", [HH, 3 * DM]); din("shb2", [128, 24])
    din("ew1", [NE, 2 * DM, HH]); din("eb1", [HH, NE])
    din("ew2mc", [DMC * HH, NE * 3 * 128])
    din("eb2", [128, NE * 24])
    din("rfw1", [DM, 2 * DM]); din("rfb1", [128, 16])
    din("rfw2", [2 * DM, DM]); din("rfb2", [128, 8])
    din("swt", [1, 1]); din("mu", [128, 8]); din("iw", [128, 8])
    din("WinT", [DM, 2 * DI]); din("WoutT", [DI, DM])
    for p in ("p1", "p2"):
        din(f"WxT_{p}", [DI, RK + 2 * NST])
        din(f"WdtT_{p}", [RK, DI])
        din(f"convw_{p}", [128, DIC * KC])
        din(f"convb_{p}", [128, DIC])
        din(f"bdt_{p}", [128, DIC])
        din(f"Alog_{p}", [128, DIC * NST])
        din(f"Dp_{p}", [128, DIC])
    dout("out", [DM, OWN])
    dscr("x2_hbm", [DM, OWN])
    dscr("dbgI", [DM, OWN]); dscr("dbgV", [DM, OWN])
    dscr("dbgSv", [DM, OWN]); dscr("dbgSx", [DM, OWN])
    dscr("dbgXsh", [DM, OWN]); dscr("dbgVsh", [DM, OWN])
    dscr("dbgHlt", [1, OWN]); dscr("dbgW", [128, OWN // 128 * 2])
    dscr("dbgSelA", [128, 2 * OWN])
    dscr("h_hbm", [DM, OWN], BF16)
    dscr("sz_hbm", [DI, OWN])
    dscr("o1_hbm", [DM, OWN])
    dscr("x3_hbm", [DM, OWN])
    dram["WinT16"] = nc.dram_tensor("WinT16", [DM, 2 * DI], BF16).ap()
    dram["WoutT16"] = nc.dram_tensor("WoutT16", [DI, DM], BF16).ap()
    dram["rfw1_16"] = nc.dram_tensor("rfw1_16", [DM, 2 * DM], BF16).ap()
    dram["rfw2_16"] = nc.dram_tensor("rfw2_16", [2 * DM, DM], BF16).ap()
    dram["ew2_16"] = nc.dram_tensor("ew2_16", [DMC * HH, NE * 3 * 128], BF16).ap()
    AGW = DIC * NST + DIC * 3
    dram["cin"] = nc.dram_tensor("cin", [128, AGW], F32).ap()
    dram["cout"] = nc.dram_tensor("cout", [256, AGW], F32).ap()

    sbuf("ident", [128, 128], F32)
    sbuf("ident16", [128, 128], BF16)
    sbuf("ones1", [1, 128], F32)
    sbuf("onesc", [128, 1], F32)
    sbuf("eps1", [1, 1], F32)
    sbuf("mhalf", [1, 1], F32)
    sbuf("iotaS", [128, SUB], F32)
    rw1_16 = MTT("rw1_16", DMC, DH, BF16)
    sbuf("rw2_16", [128, 2 * NE], BF16)
    hw1_16 = MTT("hw1_16", DMC, DH, BF16)
    sbuf("hw2f", [128, 2], F32)
    shw1_16 = MTT("shw1_16", 16, HH, BF16)
    sbuf("shw2_16", [HH, 3 * DM], BF16)
    ew1_16 = MTT("ew1_16", NE * 16, HH, BF16)
    WxT16 = {p: MTT(f"WxT16_{p}", DIC, RK + 2 * NST, BF16) for p in ("p1", "p2")}
    for p in ("p1", "p2"):
        sbuf(f"WdtT16_{p}", [RK, DI], BF16)
    for name, shape in [("rb1s", [128, 2]), ("rb2s", [NE, 1]), ("hb1s", [128, 2]),
                        ("hb2s", [1, 1]), ("shb1s", [HH, 1]), ("shb2s", [128, 24]),
                        ("eb1s", [HH, NE]), ("eb2s", [128, NE * 24]),
                        ("rfb1s", [128, 16]), ("rfb2s", [128, 8]),
                        ("mus", [128, 8]), ("iws", [128, 8]),
                        ("swsc", [1, 1]), ("swcol", [128, 1]), ("onemsw", [128, 1]),
                        ("selm", [128, 2])]:
        sbuf(name, shape, F32)
    for p in ("p1", "p2"):
        for name, w in [("convw", DIC * KC), ("convb", DIC), ("bdt", DIC),
                        ("A", DIC * NST), ("Dp", DIC)]:
            sbuf(f"s_{name}_{p}", [128, w], F32)
    sbuf("state", [128, DIC * NST], F32)
    sbuf("carry", [128, DIC * 3], F32)
    sbuf("ago0", [128, AGW], F32)
    sbuf("ago1", [128, AGW], F32)
    sb["agin"] = sb["ago0"]
    for i in range(8):
        sb[f"P{i}"] = gstack.enter_context(nc.psum_tensor(f"P{i}", [128, 512], F32))

    def pap(i, c0=0, c1=512, p0=0, p1=128):
        return sb[f"P{i}"].ap()[p0:p1, c0:c1]

    def dma(dst, src, wbuf, rbufs=(), tag="dma", accum=False, slow=False, chan=None):
        kw = dict(writes_accum=[B(wbuf)]) if accum else dict(writes=[B(wbuf)])
        ir.op("sync", lambda e, d=dst, s=src, sl=slow:
                  e.dma_start(out=d, in_=s, allow_slow_non_contiguous=True) if sl
                  else e.dma_start(out=d, in_=s),
              reads=[B(r) for r in rbufs], dma=True, tag=tag, channel=chan, **kw)

    def mm(out, lhsT, rhs, rbufs, wbuf, start, stop, tag="mm"):
        if start:
            kw = dict(writes=[B(wbuf)])
        elif stop:
            kw = dict(writes_accum=[B(wbuf)])
        else:
            kw = {}
        ir.op("tensor",
              lambda e, o=out, l=lhsT, r=rhs, s0=start, s1=stop:
                  e.matmul(o, l, r, start=s0, stop=s1),
              reads=[B(r) for r in rbufs], tag=tag, **kw)

    def trans(out, in_, rbufs, wbuf, np_=128, tag="tr"):
        ir.op("tensor",
              lambda e, o=out, i=in_, d=ap("ident")[0:np_, 0:np_]: e.transpose(o, i, d),
              reads=[B(r) for r in rbufs] + [B("ident")], writes=[B(wbuf)], tag=tag)

    def act(out, in_, func, rbufs, wbuf, bias=None, scale=None, tag="act", accum=False):
        kw = {}
        if bias is not None:
            kw["bias"] = bias
        if scale is not None:
            kw["scale"] = scale
        wkw = dict(writes_accum=[B(wbuf)]) if accum else dict(writes=[B(wbuf)])
        ir.op("scalar",
              lambda e, o=out, i=in_, f=func, k=dict(kw):
                  e.activation(out=o, in_=i, func=f, **k),
              reads=[B(r) for r in rbufs], tag=tag, **wkw)

    def scop(out, in_, rbufs, wbuf, tag="scop", accum=False):
        wkw = dict(writes_accum=[B(wbuf)]) if accum else dict(writes=[B(wbuf)])
        ir.op("scalar", lambda e, o=out, i=in_: e.copy(out=o, in_=i),
              reads=[B(r) for r in rbufs], tag=tag, **wkw)

    def vgen(emit, rbufs, wbuf, tag="v", engine="vector", accum=False):
        wkw = dict(writes_accum=[B(wbuf)]) if accum else dict(writes=[B(wbuf)])
        ir.op(engine, emit, reads=[B(r) for r in rbufs], tag=tag, **wkw)

    def tt(out, a, bb, op, rbufs, wbuf, tag="tt", engine="vector", accum=False):
        vgen(lambda e, o=out, x=a, y=bb, p=op: e.tensor_tensor(out=o, in0=x, in1=y, op=p),
             rbufs, wbuf, tag, engine, accum)

    def tsc(out, in_, s1, op0, rbufs, wbuf, s2=None, op1=None, tag="ts", accum=False,
            engine="vector"):
        def em(e, o=out, i=in_, a=s1, b2=s2, p0=op0, p1=op1):
            if p1 is not None:
                return e.tensor_scalar(out=o, in0=i, scalar1=a, scalar2=b2,
                                       op0=p0, op1=p1)
            return e.tensor_scalar(out=o, in0=i, scalar1=a, scalar2=None, op0=p0)
        vgen(em, rbufs, wbuf, tag, engine, accum)

    def stt(out, in0, scalar, in1, op0, op1, rbufs, wbuf, tag="stt", accum=False,
            engine="vector"):
        vgen(lambda e, o=out, a=in0, s=scalar, b2=in1, p0=op0, p1=op1:
                 e.scalar_tensor_tensor(out=o, in0=a, scalar=s, in1=b2, op0=p0, op1=p1),
             rbufs, wbuf, tag, engine, accum)

    def vcp(out, in_, rbufs, wbuf, tag="vcp", accum=False, engine="vector"):
        vgen(lambda e, o=out, i=in_: e.tensor_copy(o, i), rbufs, wbuf, tag,
             engine, accum)

    def softplus(out, x_ap, bias_col, rbufs, wbuf, t1, t2, n1, n2, tag="sp",
                 engine="vector", accum=False):
        act(t2, x_ap, AF.Relu, list(rbufs), n2, bias=bias_col, tag=tag + "r")
        stt(out, x_ap, bias_col, t2, OP.add, OP.subtract, list(rbufs) + [n2],
            wbuf, tag=tag + "m", accum=accum)
        act(out, out, AF.Exp, [wbuf], wbuf, tag=tag + "e", accum=True)
        act(out, out, AF.Ln, [wbuf], wbuf, bias=1.0, scale=1.0, tag=tag + "l",
            accum=True)
        tt(out, out, t2, OP.add, [wbuf, n2], wbuf, tag=tag + "a", engine=engine,
           accum=True)

    def memset(apx, val, wbuf, tag="ms", accum=False):
        vgen(lambda e, o=apx, v=val: e.memset(o, v), [], wbuf, tag, "vector", accum)

    pstack = contextlib.ExitStack()
    sbuf("stage", [128, 3072], F32, stack=pstack)
    sbuf("cst", [128, 3072], BF16, stack=pstack)
    ir.op("gpsimd", lambda e: e.iota(ap("ident"), [[1, 128]], channel_multiplier=-1,
                                     base=0, allow_small_or_imprecise_dtypes=True),
          writes=[B("ident")], tag="iota_id")
    tsc(ap("ident"), ap("ident"), 0.0, OP.is_equal, ["ident"], "ident", tag="id_eq")
    vcp(ap("ident16"), ap("ident"), ["ident"], "ident16", tag="id16")
    memset(ap("ones1"), 1.0, "ones1")
    memset(ap("onesc"), 1.0, "onesc")
    memset(ap("eps1"), 1e-5, "eps1")
    memset(ap("mhalf"), -0.5, "mhalf")
    ir.op("gpsimd", lambda e: e.iota(ap("iotaS"), [[128, SUB]], channel_multiplier=1,
                                     base=0, allow_small_or_imprecise_dtypes=True),
          writes=[B("iotaS")], tag="iota_s")

    for sn, dn in [("rb1s", "rb1"), ("rb2s", "rb2"), ("hb1s", "hb1"), ("hb2s", "hb2"),
                   ("shb1s", "shb1"), ("shb2s", "shb2"), ("eb1s", "eb1"),
                   ("eb2s", "eb2"), ("rfb1s", "rfb1"), ("rfb2s", "rfb2"),
                   ("mus", "mu"), ("iws", "iw"), ("selm", "selmask"),
                   ("swsc", "swt")]:
        dma(ap(sn), dram[dn], sn, tag=f"ld_{sn}", chan="ldp")
    for p in ("p1", "p2"):
        for nm in ("convw", "convb", "bdt", "Dp"):
            dma(ap(f"s_{nm}_{p}"), dram[f"{nm}_{p}"], f"s_{nm}_{p}",
                tag=f"ld{nm}{p}", chan="ldp")
        dma(ap(f"s_A_{p}"), dram[f"Alog_{p}"], f"s_A_{p}", tag=f"ldA{p}",
            chan="ldp")

    def cast_res(src_dram, rows, cols, bufname, chunk_apfn):
        nch = max(1, rows // 128)
        prow = min(rows, 128)
        for ch in range(nch):
            sa_ = ap("stage")[0:prow, 0:cols]
            dma(sa_, src_dram[ch * prow:(ch + 1) * prow, 0:cols], "stage",
                tag=f"ldc_{bufname}{ch}")
            vcp(chunk_apfn(ch), sa_, ["stage"], bufname, tag=f"c_{bufname}{ch}",
                accum=(ch > 0))

    cast_res(dram["rw1"], DM, DH, "rw1_16", lambda ch: rw1_16[ch])
    cast_res(dram["rw2"], DH, NE, "rw2_16",
             lambda ch: ap("rw2_16")[:, ch * NE:(ch + 1) * NE])
    cast_res(dram["hw1"], DM, DH, "hw1_16", lambda ch: hw1_16[ch])
    cast_res(dram["hw2"], DH, 1, "hw2f", lambda ch: ap("hw2f")[:, ch:ch + 1])
    cast_res(dram["shw1"], 2 * DM, HH, "shw1_16", lambda ch: shw1_16[ch])
    cast_res(dram["shw2"], HH, 3 * DM, "shw2_16", lambda ch: ap("shw2_16"))
    for p in ("p1", "p2"):
        cast_res(dram[f"WxT_{p}"], DI, RK + 2 * NST, f"WxT16_{p}",
                 lambda ch, pp=p: WxT16[pp][ch])
        cast_res(dram[f"WdtT_{p}"], RK, DI, f"WdtT16_{p}",
                 lambda ch, pp=p: ap(f"WdtT16_{pp}"))
    for e_ in range(NE):
        for ch in range(16):
            sa_ = ap("stage")[:, 0:HH]
            dma(sa_, dram["ew1"][e_, ch * 128:(ch + 1) * 128, :], "stage",
                tag=f"lew1_{e_}_{ch}")
            vcp(ew1_16[e_ * 16 + ch], sa_, ["stage"], "ew1_16",
                tag=f"cew1_{e_}_{ch}", accum=(e_ + ch > 0))

    def cast_hbm(dst, src, rows, cols):
        step = min(cols, 3072)
        first = True
        for ch in range((rows + 127) // 128):
            prow = min(128, rows - ch * 128)
            for c0 in range(0, cols, step):
                w_ = min(step, cols - c0)
                sa_ = ap("stage")[0:prow, 0:w_]
                dma(sa_, dram[src][ch * 128:ch * 128 + prow, c0:c0 + w_], "stage",
                    tag=f"lh_{dst}{ch}_{c0}")
                ca = ap("cst")[0:prow, 0:w_]
                vcp(ca, sa_, ["stage"], "cst", tag=f"ch_{dst}{ch}_{c0}")
                dma(dram[dst][ch * 128:ch * 128 + prow, c0:c0 + w_], ca, dst,
                    rbufs=["cst"], tag=f"sh_{dst}{ch}_{c0}", accum=not first)
                first = False

    cast_hbm("WinT16", "WinT", DM, 2 * DI)
    cast_hbm("WoutT16", "WoutT", DI, DM)
    cast_hbm("rfw1_16", "rfw1", DM, 2 * DM)
    cast_hbm("rfw2_16", "rfw2", 2 * DM, DM)
    cast_hbm("ew2_16", "ew2mc", DMC * HH, NE * 3 * 128)
    ir.barrier("bpro")
    pstack.close()
    for p in ("p1", "p2"):
        act(ap(f"s_A_{p}"), ap(f"s_A_{p}"), AF.Exp, [f"s_A_{p}"], f"s_A_{p}",
            tag="aexp")
        tsc(ap(f"s_A_{p}"), ap(f"s_A_{p}"), -1.0, OP.mult, [f"s_A_{p}"], f"s_A_{p}",
            tag="aneg")
    act(ap("swsc"), ap("swsc"), AF.Sigmoid, ["swsc"], "swsc", tag="swsig")
    mm(pap(0, 0, 1), ap("ones1"), ap("swsc"), ["ones1", "swsc"], "P0", True, True,
       tag="swb")
    scop(ap("swcol"), pap(0, 0, 1), ["P0"], "swcol", tag="swev")
    tsc(ap("onemsw"), ap("swcol"), -1.0, OP.mult, ["swcol"], "onemsw",
        s2=1.0, op1=OP.add, tag="onemsw")

    _uid = [0]

    def emit_ln(src_mt, srcname, dst32_mt, d32name, dst16_mt, d16name, T, inplace,
                lnrow, lnrow2, lntmp):
        k = _uid[0]; _uid[0] += 1
        srcn = (lambda q: srcname[q]) if isinstance(srcname, list) else (lambda q: srcname)
        dstn = (lambda q: d32name[q]) if isinstance(d32name, list) else (lambda q: d32name)
        nspl = (T + 511) // 512
        for s in range(nspl):
            c0, c1 = s * 512, min(T, (s + 1) * 512)
            for mc2 in range(DMC):
                mm(pap(3, c0, c1, 0, 1), ap("onesc"), src_mt.cols(mc2, c0, c1),
                   ["onesc", srcn(mc2)], "P3", mc2 == 0, mc2 == DMC - 1, tag=f"lnS{k}")
            for mc2 in range(DMC):
                act(ap(lntmp)[:, 0:c1 - c0], src_mt.cols(mc2, c0, c1), AF.Square,
                    [srcn(mc2)], lntmp, tag=f"lnQ{k}")
                mm(pap(6, c0, c1, 0, 1), ap("onesc"), ap(lntmp)[:, 0:c1 - c0],
                   ["onesc", lntmp], "P6", mc2 == 0, mc2 == DMC - 1, tag=f"lnQ2{k}")
        tsc(ap(lnrow)[:, 0:T], pap(3, 0, T, 0, 1), 1.0 / DM, OP.mult, ["P3"],
            lnrow, tag=f"lnM{k}")
        tsc(ap(lnrow2)[:, 0:T], pap(6, 0, T, 0, 1), 1.0 / DM, OP.mult, ["P6"],
            lnrow2, tag=f"lnV{k}")
        tt(ap(lntmp)[0:1, 0:T], ap(lnrow)[:, 0:T], ap(lnrow)[:, 0:T], OP.mult,
           [lnrow], lntmp, tag=f"lnm2{k}")
        tt(ap(lnrow2)[:, 0:T], ap(lnrow2)[:, 0:T], ap(lntmp)[0:1, 0:T],
           OP.subtract, [lnrow2, lntmp], lnrow2, tag=f"lnv2{k}")
        act(ap(lnrow2)[:, 0:T], ap(lnrow2)[:, 0:T], AF.Ln, [lnrow2, "eps1"],
            lnrow2, bias=ap("eps1"), scale=1.0, tag=f"lnl{k}")
        act(ap(lnrow2)[:, 0:T], ap(lnrow2)[:, 0:T], AF.Exp, [lnrow2, "mhalf"],
            lnrow2, scale=ap("mhalf"), tag=f"lne{k}")
        tt(ap(lnrow)[:, 0:T], ap(lnrow)[:, 0:T], ap(lnrow2)[:, 0:T], OP.mult,
           [lnrow, lnrow2], lnrow, tag=f"lnmr{k}")
        for s in range(nspl):
            c0, c1 = s * 512, min(T, (s + 1) * 512)
            mm(pap(3, c0, c1), ap("ones1"), ap(lnrow2)[:, c0:c1],
               ["ones1", lnrow2], "P3", True, True, tag=f"lnB1{k}")
            mm(pap(6, c0, c1), ap("ones1"), ap(lnrow)[:, c0:c1],
               ["ones1", lnrow], "P6", True, True, tag=f"lnB2{k}")
        for mc2 in range(DMC):
            tt(ap(lntmp)[:, 0:T], src_mt[mc2], pap(3, 0, T), OP.mult,
               [srcn(mc2), "P3"], lntmp, tag=f"lnA{k}")
            tt(dst32_mt[mc2], ap(lntmp)[:, 0:T], pap(6, 0, T), OP.subtract,
               [lntmp, "P6"], dstn(mc2), tag=f"lnO{k}",
               accum=(inplace or (not isinstance(d32name, list) and mc2 > 0)))
            if dst16_mt is not None:
                vcp(dst16_mt[mc2], dst32_mt[mc2], [dstn(mc2)], d16name,
                    tag=f"lnC{k}", accum=(mc2 > 0))

    def emit_ffn(src_name, dst_name, fcall):
        fstack = contextlib.ExitStack()
        sx = f"q{fcall}_"
        xin = MTT(sx + "xin", DMC, TF, stack=fstack)
        xf32 = MTT(sx + "xf32", DMC, TF, stack=fstack)
        xf16 = MTT(sx + "xf16", DMC, TF, BF16, stack=fstack)
        If = MTT(sx + "If", DMC, TF, stack=fstack)
        Vf = MTT(sx + "Vf", DMC, TF, stack=fstack)
        I16 = MTT(sx + "I16", DMC, TF, BF16, stack=fstack)
        V16 = MTT(sx + "V16", DMC, TF, BF16, stack=fstack)
        h116 = MTT(sx + "h116", 2, TF, BF16, stack=fstack)
        hh16 = MTT(sx + "hh16", 2, TF, BF16, stack=fstack)
        he16 = MTT(sx + "he16", NE, TF, BF16, prow=HH, stack=fstack)
        hr16 = MTT(sx + "hr16", 16, TF, BF16, stack=fstack)
        for nm, shape, dt in [
                ("logit", [NE, TF], F32), ("tm8", [128, NE], F32),
                ("topv8", [128, NE], F32), ("topi8", [128, NE], U32),
                ("topif", [128, 2], F32), ("wslot", [128, 2 * SUB], F32),
                ("mrow", [1, 2 * TF], F32), ("tb", [64, 2 * TF], BF16),
                ("m8", [8, 2 * TF], BF16), ("hem", [64, NE * 2 * TF], BF16),
                ("eb2Th", [8, 24 * 128], BF16), ("eb2Tl", [8, 24 * 128], BF16),
                ("wfm0", [1, TF], F32), ("wfm1", [1, TF], F32),
                ("hs16", [HH, TF], BF16),
                ("hlt", [1, TF], F32), ("x2st", [128, 2 * TF], F32),
                ("wb0s", [128, TF], F32), ("wb1s", [128, TF], F32),
                ("hbs", [128, TF], F32),
                ("lnrow", [1, 512], F32), ("lnrow2", [1, 512], F32),
                ("lntmp", [128, 512], F32),
                ("wt1", [128, 2 * DMC * 128], BF16),
                ("wt2", [128, 2 * 16 * 128], BF16)] + [
                (nm + sfx, shape, dt) for sfx in ("A", "B") for nm, shape, dt in [
                    ("sela", [128, 2 * TF], F32), ("selb", [128, 2 * TF], F32),
                    ("selg", [128, 2 * TF], F32),

                    ("vn0", [128, TF], F32), ("vn1", [128, TF], F32),
                    ("svt", [128, TF], F32), ("sxt", [128, TF], F32),
                    ("vsht", [128, TF], F32), ("xsht", [128, TF], F32),
                    ("xmu", [128, TF], F32),
                    ("ffA", [128, TF], F32), ("ffB", [128, TF], F32),
                    ("ew2b", [HH, NE * 3 * 128], BF16)]]:
            sbuf(sx + nm, shape, dt, stack=fstack)

        def a_(n):
            return ap(sx + n)

        def n_(n):
            return sx + n

        src_h = dram[src_name]
        dst_h = dram[dst_name]
        for part in range(3):
            for mc in range(DMC):
                pm = part * 8 + mc
                trans(pap(6, 0, 128, 0, 8),
                      bass.AP(sb["eb2s"], part * 8 + mc, [[NE * 24, 128], [24, NE]]),
                      ["eb2s"], "P6", np_=128, tag=f"Q{fcall}ebt")
                scop(a_("eb2Th")[:, pm * 128:(pm + 1) * 128],
                     pap(6, 0, 128, 0, 8), ["P6"], n_("eb2Th"),
                     tag=f"Q{fcall}ebh", accum=(pm > 0))
                tt(a_("eb2Tl")[:, pm * 128:(pm + 1) * 128],
                   pap(6, 0, 128, 0, 8),
                   a_("eb2Th")[:, pm * 128:(pm + 1) * 128], OP.subtract,
                   ["P6", n_("eb2Th")], n_("eb2Tl"), tag=f"Q{fcall}ebl",
                   accum=(pm > 0))
        for t_ in range(NFT):
            tc = slice(t_ * TF, (t_ + 1) * TF)
            tg = f"F{fcall}T{t_}"
            for mc in range(DMC):
                dma(xin[mc], src_h[mc * 128:(mc + 1) * 128, tc], n_("xin"),
                    rbufs=([src_name] if src_name != "xloc" else []),
                    tag=tg + f"lx{mc}", accum=(mc > 0))
            if FFN_UPTO < 1:
                continue
            emit_ln(xin, n_("xin"), xf32, n_("xf32"), xf16, n_("xf16"), TF, False,
                    n_("lnrow"), n_("lnrow2"), n_("lntmp"))
            if FFN_UPTO < 2:
                continue
            for mc2 in range(2):
                ph = 0 if mc2 == 0 else 7
                for kc in range(DMC):
                    mm(pap(ph, 0, TF, 0, 128),
                       rw1_16.cols(kc, mc2 * 128, (mc2 + 1) * 128), xf16[kc],
                       ["rw1_16", n_("xf16")], f"P{ph}", kc == 0, kc == DMC - 1,
                       tag=tg + "rh")
                act(h116[mc2], pap(ph, 0, TF, 0, 128), AF.Gelu, [f"P{ph}"],
                    n_("h116"), bias=ap("rb1s")[:, mc2:mc2 + 1], tag=tg + "rg",
                    accum=(mc2 > 0))
            for kc2 in range(2):
                mm(pap(1, 0, TF, 0, NE), ap("rw2_16")[:, kc2 * NE:(kc2 + 1) * NE],
                   h116[kc2], ["rw2_16", n_("h116")], "P1", kc2 == 0, kc2 == 1,
                   tag=tg + "rl")
            act(a_("logit")[:, 0:TF], pap(1, 0, TF, 0, NE), AF.Identity, ["P1"],
                n_("logit"), bias=ap("rb2s"), tag=tg + "rle")
            if FFN_UPTO < 3:
                continue
            for s in range(SUB):
                trans(pap(6, 0, NE, 0, 128), a_("logit")[:, s * 128:(s + 1) * 128],
                      [n_("logit")], "P6", np_=NE, tag=tg + "ttr")
                scop(a_("tm8"), pap(6, 0, NE, 0, 128), ["P6"], n_("tm8"),
                     tag=tg + "tme")
                vgen(lambda e: e.max(a_("topv8"), a_("tm8")),
                     [n_("tm8")], n_("topv8"), tag=tg + "mwi", accum=(s > 0))
                vgen(lambda e: e.max_index(a_("topi8"), a_("topv8"), a_("tm8")),
                     [n_("tm8"), n_("topv8")], n_("topi8"), tag=tg + "mwi2",
                     accum=(s > 0))
                tt(a_("ffAA")[:, 0:1], a_("topv8")[:, 0:1], a_("topv8")[:, 1:2],
                   OP.subtract, [n_("topv8")], n_("ffAA"), tag=tg + "d12")
                act(a_("wslot")[:, 2 * s:2 * s + 1], a_("ffAA")[:, 0:1], AF.Sigmoid,
                    [n_("ffAA")], n_("wslot"), tag=tg + "w0", accum=(s > 0))
                act(a_("wslot")[:, 2 * s + 1:2 * s + 2], a_("ffAA")[:, 0:1],
                    AF.Sigmoid, [n_("ffAA")], n_("wslot"), scale=-1.0, tag=tg + "w1",
                    accum=True)
                vcp(a_("topif"), a_("topi8")[:, 0:2], [n_("topi8")], n_("topif"),
                    tag=tg + "tif")
                for k_ in range(2):
                    trans(pap(6, 0, 128, 0, 1), a_("topif")[:, k_:k_ + 1],
                          [n_("topif")], "P6", np_=128, tag=tg + "mtr")
                    scop(a_("mrow")[0:1, k_ * TF + s * 128:k_ * TF + s * 128 + 128],
                         pap(6, 0, 128, 0, 1), ["P6"], n_("mrow"),
                         tag=tg + "mre", accum=(s + k_ > 0))
            for s in range(SUB):
                for k_ in range(2):
                    trans(pap(6, 0, 128, 0, 1),
                          a_("wslot")[:, 2 * s + k_:2 * s + k_ + 1],
                          [n_("wslot")], "P6", np_=128, tag=tg + "wtr")
                    scop(a_(f"wfm{k_}")[0:1, s * 128:(s + 1) * 128],
                         pap(6, 0, 128, 0, 1), ["P6"],
                         n_(f"wfm{k_}"), tag=tg + "wfm", accum=(s > 0))
            if FFN_UPTO < 4:
                continue
            for s2 in range((TF + 511) // 512):
                c0, c1 = s2 * 512, min(TF, (s2 + 1) * 512)
                mm(pap(4, c0, c1), ap("ones1"), a_("wfm0")[0:1, c0:c1],
                   ["ones1", n_("wfm0")], "P4", True, True, tag=tg + "wb0")
                mm(pap(5, c0, c1), ap("ones1"), a_("wfm1")[0:1, c0:c1],
                   ["ones1", n_("wfm1")], "P5", True, True, tag=tg + "wb1")
            scop(a_("wb0s"), pap(4, 0, TF), ["P4"], n_("wb0s"), tag=tg + "wbe0")
            scop(a_("wb1s"), pap(5, 0, TF), ["P5"], n_("wb1s"), tag=tg + "wbe1")
            for k_ in range(2):
                mm(pap(6, 0, TF, 0, 64), ap("ones1")[0:1, 0:64],
                   a_("mrow")[0:1, k_ * TF:(k_ + 1) * TF], ["ones1", n_("mrow")],
                   "P6", True, True, tag=tg + "tbb")
                scop(a_("tb")[:, k_ * TF:(k_ + 1) * TF], pap(6, 0, TF, 0, 64),
                     ["P6"], n_("tb"), tag=tg + "tbe", accum=(k_ > 0))
                tsc(a_("m8")[:, k_ * TF:(k_ + 1) * TF],
                    a_("tb")[0:8, k_ * TF:(k_ + 1) * TF],
                    ap("iotaS")[0:8, 0:1], OP.is_equal, [n_("tb"), "iotaS"],
                    n_("m8"), tag=tg + "m8", accum=(k_ > 0))
            memset(Vf.ap(), 0.0, n_("Vf"), tag=tg + "v0")
            memset(V16.ap(), 0.0, n_("V16"), tag=tg + "v160")
            if FFN_UPTO < 5:
                continue
            for it in range(1 if FFN_UPTO < 6 else 2):
                Imt, Iname = (xf32, n_("xf32")) if it == 0 else (If, n_("If"))
                Im16, I16n = (xf16, n_("xf16")) if it == 0 else (I16, n_("I16"))
                nkc = DMC if it == 0 else 2 * DMC

                def ctx16(kc):
                    if kc < DMC:
                        return Im16[kc], I16n
                    return V16[kc - DMC], n_("V16")

                if "hsh" in FFN_SK:
                    continue
                for kc in range(nkc):
                    rhs, rn = ctx16(kc)
                    mm(pap(0, 0, TF, 0, HH), shw1_16[kc], rhs, ["shw1_16", rn], "P0",
                       kc == 0, kc == nkc - 1, tag=tg + "sh1")
                act(a_("hs16")[:, 0:TF], pap(0, 0, TF, 0, HH), AF.Gelu, ["P0"],
                    n_("hs16"), bias=ap("shb1s"), tag=tg + "shg")
                if "hex" in FFN_SK:
                    continue
                for e_ in range(NE):
                    pe = 0 if e_ % 2 == 0 else 7
                    for kc in range(nkc):
                        rhs, rn = ctx16(kc)
                        mm(pap(pe, 0, TF, 0, HH), ew1_16[e_ * 16 + kc], rhs,
                           ["ew1_16", rn], f"P{pe}", kc == 0, kc == nkc - 1,
                           tag=tg + "eh")
                    act(he16[e_], pap(pe, 0, TF, 0, HH), AF.Gelu, [f"P{pe}"],
                        n_("he16"), bias=ap("eb1s")[:, e_:e_ + 1], tag=tg + "ehg",
                        accum=(e_ > 0))
                for k_ in range(2):
                    for e_ in range(NE):
                        stt(a_("hem")[:, (k_ * NE + e_) * TF:
                                      (k_ * NE + e_ + 1) * TF],
                            a_("tb")[:, k_ * TF:(k_ + 1) * TF], float(e_),
                            he16[e_], OP.is_equal, OP.mult,
                            [n_("tb"), n_("he16")], n_("hem"), tag=tg + "hm",
                            accum=(k_ + e_ > 0))
                if "halt" in FFN_SK:
                    continue
                for mc2 in range(2):
                    ph = 0 if mc2 == 0 else 7
                    for kc in range(DMC):
                        mm(pap(ph, 0, TF, 0, 128),
                           hw1_16.cols(kc, mc2 * 128, (mc2 + 1) * 128), Im16[kc],
                           ["hw1_16", I16n], f"P{ph}", kc == 0, kc == DMC - 1,
                           tag=tg + "hh")
                    act(hh16[mc2], pap(ph, 0, TF, 0, 128), AF.Gelu, [f"P{ph}"],
                        n_("hh16"), bias=ap("hb1s")[:, mc2:mc2 + 1], tag=tg + "hg",
                        accum=(mc2 > 0))
                for kc2 in range(2):
                    tsc(a_("ffAA")[:, 0:TF], hh16[kc2], ap("hw2f")[:, kc2:kc2 + 1],
                        OP.mult, [n_("hh16"), "hw2f"], n_("ffAA"), tag=tg + "hlx")
                    mm(pap(1, 0, TF, 0, 1), ap("onesc"), a_("ffAA")[:, 0:TF],
                       ["onesc", n_("ffAA")], "P1", kc2 == 0, kc2 == 1,
                       tag=tg + "hl")
                act(a_("hlt")[:, 0:TF], pap(1, 0, TF, 0, 1), AF.Sigmoid, ["P1"],
                    n_("hlt"), bias=ap("hb2s"), tag=tg + "hs")
                for s2 in range((TF + 511) // 512):
                    c0, c1 = s2 * 512, min(TF, (s2 + 1) * 512)
                    mm(pap(6, c0, c1), ap("ones1"), a_("hlt")[:, c0:c1],
                       ["ones1", n_("hlt")], "P6", True, True, tag=tg + "hb")
                scop(a_("hbs"), pap(6, 0, TF), ["P6"], n_("hbs"), tag=tg + "hbe")
                if "mc1" in FFN_SK:
                    continue
                for mc in range(DMC):
                    sfx = "A" if mc % 2 == 0 else "B"
                    ge = "vector" if mc % 2 == 0 else "gpsimd"
                    eb1, eb2_ = (1, 3) if mc % 2 == 0 else (0, 7)

                    def s_(nm):
                        return a_(nm + sfx)

                    def sn(nm):
                        return n_(nm + sfx)

                    dma(s_("ew2b"), dram["ew2_16"][mc * HH:(mc + 1) * HH, :],
                        sn("ew2b"), rbufs=["ew2_16"], tag=tg + "e2w",
                        accum=(it + mc // 2 > 0))
                    tsc(s_("xmu"), Imt[mc], ap("mus")[:, mc:mc + 1], OP.subtract,
                        [Iname, "mus"], sn("xmu"), tag=tg + "xmu")
                    for part in range(3):
                        if it == 0 and part == 0:
                            continue
                        pp = eb1 if part % 2 == 0 else eb2_
                        fc = part * 8 + mc
                        mm(pap(pp, 0, TF, 0, 128),
                           ap("shw2_16")[:, fc * 128:(fc + 1) * 128],
                           a_("hs16")[:, 0:TF], ["shw2_16", n_("hs16")], f"P{pp}",
                           True, True, tag=tg + "sh2")
                        bias_c = ap("shb2s")[:, fc:fc + 1]
                        if part == 0:
                            act(s_("vsht"), pap(pp, 0, TF, 0, 128), AF.Sigmoid,
                                [f"P{pp}"], sn("vsht"), bias=bias_c, tag=tg + "ssa")
                        elif part == 1:
                            softplus(s_("xsht"), pap(pp, 0, TF, 0, 128), bias_c,
                                     [f"P{pp}", "shb2s"], sn("xsht"),
                                     s_("ffA")[:, 0:TF], s_("ffB")[:, 0:TF],
                                     sn("ffA"), sn("ffB"), tag=tg + "ssb",
                                     engine=ge)
                        else:
                            act(s_("vn1"), pap(pp, 0, TF, 0, 128), AF.Sigmoid,
                                [f"P{pp}"], sn("vn1"), bias=bias_c, tag=tg + "ssg")
                    if "mc2" in FFN_SK:
                        continue
                    if it == 0:
                        stt(s_("svt"), s_("xsht"), -1.0, s_("xmu"), OP.mult, OP.mult,
                            [sn("xsht"), sn("xmu")], sn("svt"), tag=tg + "vs0")
                    else:
                        tt(s_("ffA")[:, 0:TF], s_("vsht"), Vf[mc], OP.mult,
                           [sn("vsht"), n_("Vf")], sn("ffA"), tag=tg + "vsa",
                           engine=ge)
                        tt(s_("ffB")[:, 0:TF], s_("xsht"), s_("xmu"), OP.mult,
                           [sn("xsht"), sn("xmu")], sn("ffB"), tag=tg + "vsb",
                           engine=ge)
                        tt(s_("svt"), s_("ffA")[:, 0:TF], s_("ffB")[:, 0:TF],
                           OP.subtract, [sn("ffA"), sn("ffB")], sn("svt"),
                           tag=tg + "vs1", engine=ge)
                    tt(s_("ffA")[:, 0:TF], s_("vn1"), s_("svt"), OP.mult,
                       [sn("vn1"), sn("svt")], sn("ffA"), tag=tg + "xs1", engine=ge)
                    stt(s_("xsht"), s_("ffA")[:, 0:TF], DT_CONST, Imt[mc], OP.mult,
                        OP.add, [sn("ffA"), Iname], sn("xsht"), tag=tg + "xs2")
                    vcp(s_("vsht"), s_("svt"), [sn("svt")], sn("vsht"),
                        tag=tg + "xs3", engine=ge)
                    if "mc3" in FFN_SK:
                        continue
                    for part, dst, dn in ((0, s_("sela"), sn("sela")),
                                          (1, s_("selb"), sn("selb")),
                                          (2, s_("selg"), sn("selg"))):
                        if it == 0 and part == 0:
                            continue
                        for k_ in range(2):
                            pp = eb1 if k_ == 0 else eb2_
                            for e_ in range(NE):
                                mm(pap(pp, 0, TF, 0, 128),
                                   s_("ew2b")[:, e_ * 384 + part * 128:
                                              e_ * 384 + (part + 1) * 128],
                                   a_("hem")[:, (k_ * NE + e_) * TF:
                                             (k_ * NE + e_ + 1) * TF],
                                   [sn("ew2b"), n_("hem")], f"P{pp}",
                                   e_ == 0, False, tag=tg + "e2")
                            pm = part * 8 + mc
                            for hl, ebt in ((0, "eb2Th"), (1, "eb2Tl")):
                                mm(pap(pp, 0, TF, 0, 128),
                                   a_(ebt)[:, pm * 128:(pm + 1) * 128],
                                   a_("m8")[:, k_ * TF:(k_ + 1) * TF],
                                   [n_(ebt), n_("m8")], f"P{pp}",
                                   False, hl == 1, tag=tg + "e2b")
                            sl = dst[:, k_ * TF:(k_ + 1) * TF]
                            if part == 0:
                                act(sl, pap(pp, 0, TF, 0, 128), AF.Sigmoid,
                                    [f"P{pp}"], dn, tag=tg + "sga",
                                    accum=(k_ > 0))
                            elif part == 2:
                                act(sl, pap(pp, 0, TF, 0, 128), AF.Sigmoid,
                                    [f"P{pp}"], dn, tag=tg + "sgg",
                                    accum=(k_ > 0))
                            else:
                                softplus(sl, pap(pp, 0, TF, 0, 128), 0.0,
                                         [f"P{pp}"], dn, None,
                                         s_("ffB")[:, 0:TF], None, sn("ffB"),
                                         tag=tg + "sgb", engine=ge,
                                         accum=(k_ > 0))
                    if "mc4" in FFN_SK:
                        continue

                    if "mc5" in FFN_SK:
                        continue
                    if it == 0:
                        stt(s_("vn0"), s_("selb")[:, 0:TF], -1.0, s_("xmu"), OP.mult,
                            OP.mult, [sn("selb"), sn("xmu")], sn("vn0"),
                            tag=tg + "cv0")
                        stt(s_("vn1"), s_("selb")[:, TF:2 * TF], -1.0, s_("xmu"),
                            OP.mult, OP.mult, [sn("selb"), sn("xmu")], sn("vn1"),
                            tag=tg + "cv1")
                    else:
                        tt(s_("vn0"), s_("sela")[:, 0:TF], Vf[mc], OP.mult,
                           [sn("sela"), n_("Vf")], sn("vn0"), tag=tg + "cva",
                           engine=ge)
                        tt(s_("ffA")[:, 0:TF], s_("selb")[:, 0:TF], s_("xmu"), OP.mult,
                           [sn("selb"), sn("xmu")], sn("ffA"), tag=tg + "cvb",
                           engine=ge)
                        tt(s_("vn0"), s_("vn0"), s_("ffA")[:, 0:TF], OP.subtract,
                           [sn("vn0"), sn("ffA")], sn("vn0"), tag=tg + "cvc",
                           engine=ge)
                        tt(s_("vn1"), s_("sela")[:, TF:2 * TF], Vf[mc], OP.mult,
                           [sn("sela"), n_("Vf")], sn("vn1"), tag=tg + "cvd",
                           engine=ge)
                        tt(s_("ffA")[:, 0:TF], s_("selb")[:, TF:2 * TF], s_("xmu"),
                           OP.mult, [sn("selb"), sn("xmu")], sn("ffA"),
                           tag=tg + "cve", engine=ge)
                        tt(s_("vn1"), s_("vn1"), s_("ffA")[:, 0:TF], OP.subtract,
                           [sn("vn1"), sn("ffA")], sn("vn1"), tag=tg + "cvf",
                           engine=ge)
                    tt(s_("svt"), s_("vn0"), a_("wb0s"), OP.mult,
                       [sn("vn0"), n_("wb0s")], sn("svt"), tag=tg + "sv0",
                       engine=ge)
                    tt(s_("ffA")[:, 0:TF], s_("vn1"), a_("wb1s"), OP.mult,
                       [sn("vn1"), n_("wb1s")], sn("ffA"), tag=tg + "sv1",
                       engine=ge)
                    tt(s_("sxt"), s_("selg")[:, 0:TF], s_("svt"), OP.mult,
                       [sn("selg"), sn("svt")], sn("sxt"), tag=tg + "sx0",
                       engine=ge)
                    tt(s_("ffB")[:, 0:TF], s_("selg")[:, TF:2 * TF],
                       s_("ffA")[:, 0:TF], OP.mult, [sn("selg"), sn("ffA")],
                       sn("ffB"), tag=tg + "sx1", engine=ge)
                    tt(s_("svt"), s_("svt"), s_("ffA")[:, 0:TF], OP.add,
                       [sn("svt"), sn("ffA")], sn("svt"), tag=tg + "sv2",
                       engine=ge)
                    tt(s_("sxt"), s_("sxt"), s_("ffB")[:, 0:TF], OP.add,
                       [sn("sxt"), sn("ffB")], sn("sxt"), tag=tg + "sx4",
                       engine=ge)

                    stt(s_("ffA")[:, 0:TF], s_("sxt"), DT_CONST, Imt[mc], OP.mult,
                        OP.add, [sn("sxt"), Iname], sn("ffA"), tag=tg + "ix")
                    tsc(s_("ffB")[:, 0:TF], s_("xsht"), ap("swcol"), OP.mult,
                        [sn("xsht"), "swcol"], sn("ffB"), tag=tg + "ib")
                    stt(s_("ffA")[:, 0:TF], s_("ffA")[:, 0:TF], ap("onemsw"),
                        s_("ffB")[:, 0:TF], OP.mult, OP.add,
                        [sn("ffA"), "onemsw", sn("ffB")], sn("ffA"), tag=tg + "ic")
                    tt(s_("ffA")[:, 0:TF], s_("ffA")[:, 0:TF], Imt[mc], OP.subtract,
                       [sn("ffA"), Iname], sn("ffA"), tag=tg + "id", engine=ge)
                    tt(s_("ffA")[:, 0:TF], s_("ffA")[:, 0:TF], a_("hbs"),
                       OP.mult, [sn("ffA"), n_("hbs")], sn("ffA"), tag=tg + "ie",
                       engine=ge)
                    tt(If[mc], Imt[mc], s_("ffA")[:, 0:TF], OP.add,
                       [Iname, sn("ffA")], n_("If"), tag=tg + "if",
                       accum=(mc > 0 or it > 0), engine=ge)
                    tsc(s_("ffA")[:, 0:TF], s_("vsht"), ap("swcol"), OP.mult,
                        [sn("vsht"), "swcol"], sn("ffA"), tag=tg + "iv1")
                    stt(s_("ffA")[:, 0:TF], s_("svt"), ap("onemsw"),
                        s_("ffA")[:, 0:TF], OP.mult, OP.add,
                        [sn("svt"), "onemsw", sn("ffA")], sn("ffA"), tag=tg + "iv2")
                    if it == 0:
                        tt(Vf[mc], s_("ffA")[:, 0:TF], a_("hbs"), OP.mult,
                           [sn("ffA"), n_("hbs")], n_("Vf"), tag=tg + "iv3",
                           accum=True, engine=ge)
                    else:
                        tt(s_("ffA")[:, 0:TF], s_("ffA")[:, 0:TF], Vf[mc],
                           OP.subtract, [sn("ffA"), n_("Vf")], sn("ffA"),
                           tag=tg + "iv4", engine=ge)
                        tt(s_("ffA")[:, 0:TF], s_("ffA")[:, 0:TF], a_("hbs"),
                           OP.mult, [sn("ffA"), n_("hbs")], sn("ffA"),
                           tag=tg + "iv5", engine=ge)
                        tt(Vf[mc], Vf[mc], s_("ffA")[:, 0:TF], OP.add,
                           [n_("Vf"), sn("ffA")], n_("Vf"), tag=tg + "iv6",
                           accum=True, engine=ge)
                if it == 0:
                    for mc in range(DMC):
                        vcp(I16[mc], If[mc], [n_("If")], n_("I16"), tag=tg + "ci",
                            accum=(mc > 0))
                        vcp(V16[mc], Vf[mc], [n_("Vf")], n_("V16"), tag=tg + "cw",
                            accum=(mc > 0))
                if it == 1 and dbg and fcall == 0:
                    for mc in range(DMC):
                        dma(dram["dbgI"][mc * 128:(mc + 1) * 128, tc], If[mc],
                            "dbgI", chan="dbgI", rbufs=[n_("If")],
                            tag=tg + "dI", accum=True)
                        dma(dram["dbgV"][mc * 128:(mc + 1) * 128, tc], Vf[mc],
                            "dbgV", chan="dbgV", rbufs=[n_("Vf")],
                            tag=tg + "dV", accum=True)
                    if False:
                        pass

            if FFN_UPTO < 7:
                continue
            for mc in range(DMC):
                vcp(I16[mc], If[mc], [n_("If")], n_("I16"), tag=tg + "cf",
                    accum=(mc > 0))
            for mc2 in range(16):
                sl_ = mc2 % 2
                wof = sl_ * DMC * 128
                pr = 0 if mc2 % 2 == 0 else 7
                dma(a_("wt1")[:, wof:wof + DMC * 128],
                    bass.AP(dram["rfw1_16"].tensor, mc2 * 128,
                            [[2 * DM, 128], [128 * 2 * DM, DMC], [1, 128]]),
                    n_("wt1") + str(sl_), rbufs=["rfw1_16"], tag=tg + "rf1w")
                for kc in range(DMC):
                    mm(pap(pr, 0, TF, 0, 128),
                       a_("wt1")[:, wof + kc * 128:wof + (kc + 1) * 128], I16[kc],
                       [n_("wt1") + str(sl_), n_("I16")],
                       f"P{pr}", kc == 0, kc == DMC - 1, tag=tg + "rf1")
                act(hr16[mc2], pap(pr, 0, TF, 0, 128), AF.Gelu, [f"P{pr}"],
                    n_("hr16"), bias=ap("rfb1s")[:, mc2:mc2 + 1], tag=tg + "rfg",
                    accum=(mc2 > 0))
            for mc in range(DMC):
                sl_ = mc % 2
                wof = sl_ * 16 * 128
                pr = 1 if mc % 2 == 0 else 3
                dma(a_("wt2")[:, wof:wof + 16 * 128],
                    bass.AP(dram["rfw2_16"].tensor, mc * 128,
                            [[DM, 128], [128 * DM, 16], [1, 128]]),
                    n_("wt2") + str(sl_), rbufs=["rfw2_16"], tag=tg + "rf2w")
                for kc in range(16):
                    mm(pap(pr, 0, TF, 0, 128),
                       a_("wt2")[:, wof + kc * 128:wof + (kc + 1) * 128], hr16[kc],
                       [n_("wt2") + str(sl_), n_("hr16")],
                       f"P{pr}", kc == 0, kc == 15, tag=tg + "rf2")
                rsfx = "A" if mc % 2 == 0 else "B"
                rfA, rfAn = a_("ffA" + rsfx), n_("ffA" + rsfx)
                act(rfA[:, 0:TF], pap(pr, 0, TF, 0, 128), AF.Identity,
                    [f"P{pr}"], rfAn, bias=ap("rfb2s")[:, mc:mc + 1],
                    tag=tg + "rfe")
                stt(rfA[:, 0:TF], rfA[:, 0:TF], ap("iws")[:, mc:mc + 1],
                    xf32[mc], OP.mult, OP.add, [rfAn, "iws", n_("xf32")],
                    rfAn, tag=tg + "rfi")
                stn = n_("x2st") + str(mc % 2)
                st_ap = a_("x2st")[:, (mc % 2) * TF:(mc % 2 + 1) * TF]
                stt(st_ap, rfA[:, 0:TF], 0.5, xin[mc], OP.mult, OP.add,
                    [rfAn, n_("xin")], stn, tag=tg + "rfo")
                dma(dst_h[mc * 128:(mc + 1) * 128, tc], st_ap, dst_name,
                    rbufs=[stn], tag=tg + "st", accum=True, chan=stn)
        fstack.close()

    def emit_mamba(p):
        w = f"p{p}"
        mstack = contextlib.ExitStack()
        sx = f"m{p}_"
        hmb = MTT(sx + "hmb", DMC, TM, BF16, stack=mstack)
        xzx = MTT(sx + "xzx", DIC, TM + KC - 1, stack=mstack)
        xcv = MTT(sx + "xcv", DIC, TM, stack=mstack)
        xcv16 = MTT(sx + "xcv16", DIC, TM, BF16, stack=mstack)
        dtb = MTT(sx + "dtb", DIC, TM, stack=mstack)
        szc = MTT(sx + "szc", DIC, TM, stack=mstack)
        gated = MTT(sx + "gated", DIC, TM, BF16, stack=mstack)
        Ball = MTT(sx + "Ball", NST, TM, BF16, stack=mstack)
        Call = MTT(sx + "Call", NST, TM, BF16, stack=mstack)
        oev = MTT(sx + "oev", DMC, TM, stack=mstack)
        x2c = oev
        x2r = MTT(sx + "x2r", 2, TM, stack=mstack) if p == 2 else None
        o1r = MTT(sx + "o1r", 2, TM, stack=mstack) if p == 2 else None
        for nm, shape, dt in [
                ("dtxs", [128, TM], F32), ("xdbl", [RK + 2 * NST, TM], F32),
                ("xdbl16", [RK, TM], BF16),
                ("dA2", [128, 2 * TM], F32), ("dB2", [128, 2 * TM], F32),
                ("hc2", [128, 2 * TM], F32), ("gq2", [128, 2 * TM], BF16),
                ("bcr", [1, 8 * TM], F32),
                ("mtA", [128, TM], F32), ("mtB", [128, TM], F32),
                ("mtC", [128, TM], F32), ("mtD", [128, TM], F32),
                ("wt1", [128, 2 * DMC * 128], BF16),
                ("wt2", [128, 2 * DIC * 128], BF16)]:
            sbuf(sx + nm, shape, dt, stack=mstack)
        if p == 1:
            for nm, shape, dt in [("hbst", [128, DMC * TM], BF16),
                                  ("lnrow", [1, 512], F32),
                                  ("lnrow2", [1, 512], F32),
                                  ("lntmp", [128, 512], F32)]:
                sbuf(sx + nm, shape, dt, stack=mstack)

        def a_(n):
            return ap(sx + n)

        def n_(n):
            return sx + n

        if p == 1:
            memset(ap("state"), 0.0, "state", tag="st0")

        def chcols(ch):
            if p == 1:
                tcols = slice(ch * TM, (ch + 1) * TM)
            else:
                tcols = slice(OWN - (ch + 1) * TM, OWN - ch * TM)
            scols = slice(ch * TM, (ch + 1) * TM)
            mcol = slice(OWN - (ch + 1) * TM, OWN - ch * TM)
            return f"M{p}C{ch}", tcols, scols, mcol

        def s0(ch):
            tg, tcols, scols, mcol = chcols(ch)
            if p == 1:
                for mc in range(DMC):
                    dma(x2c[mc], dram["x2_hbm"][mc * 128:(mc + 1) * 128, tcols],
                        n_("oev") + str(mc), rbufs=["x2_hbm"], tag=tg + "lx2")
                emit_ln(x2c, [n_("oev") + str(q) for q in range(DMC)],
                        x2c, [n_("oev") + str(q) for q in range(DMC)],
                        hmb, n_("hmb"), TM, True,
                        n_("lnrow"), n_("lnrow2"), n_("lntmp"))
                for mc in range(DMC):
                    hb = a_("hbst")[:, mc * TM:(mc + 1) * TM]
                    vcp(hb, hmb[mc][:, ::-1], [n_("hmb")], n_("hbst") + str(mc),
                        tag=tg + "hrv")
                    dma(dram["h_hbm"][mc * 128:(mc + 1) * 128, mcol], hb, "h_hbm",
                        rbufs=[n_("hbst") + str(mc)], tag=tg + "sh", accum=True,
                        chan=n_("hbst") + str(mc))
            else:
                for mc in range(DMC):
                    dma(hmb[mc], dram["h_hbm"][mc * 128:(mc + 1) * 128, scols],
                        n_("hmb"), rbufs=["h_hbm"], tag=tg + "lh", accum=(mc > 0))

        def s1x(ch, elist):
            tg, _, _, _ = chcols(ch)
            for ec in elist:
                pb = 4 + (ec % 2)
                sl_ = ec % 2
                wof = sl_ * DMC * 128
                dma(a_("wt1")[:, wof:wof + DMC * 128],
                    bass.AP(dram["WinT16"].tensor, ec * 128,
                            [[2 * DI, 128], [128 * 2 * DI, DMC], [1, 128]]),
                    n_("wt1") + str(sl_), rbufs=["WinT16"], tag=tg + "ipw")
                for kc in range(DMC):
                    mm(pap(pb, 0, TM, 0, 128),
                       a_("wt1")[:, wof + kc * 128:wof + (kc + 1) * 128], hmb[kc],
                       [n_("wt1") + str(sl_), n_("hmb")],
                       f"P{pb}", kc == 0, kc == DMC - 1, tag=tg + "ip")
                scop(xzx.cols(ec, KC - 1, KC - 1 + TM), pap(pb, 0, TM),
                     [f"P{pb}"], n_("xzx"), tag=tg + "ipe", accum=(ec > 0))

        def s1z(ch):
            tg, tcols, scols, mcol = chcols(ch)
            if p == 1:
                for ec in range(DIC, 2 * DIC):
                    pb = 4 + (ec % 2)
                    sl_ = ec % 2
                    wof = sl_ * DMC * 128
                    dma(a_("wt1")[:, wof:wof + DMC * 128],
                        bass.AP(dram["WinT16"].tensor, ec * 128,
                                [[2 * DI, 128], [128 * 2 * DI, DMC], [1, 128]]),
                        n_("wt1") + str(sl_), rbufs=["WinT16"], tag=tg + "ipw")
                    for kc in range(DMC):
                        mm(pap(pb, 0, TM, 0, 128),
                           a_("wt1")[:, wof + kc * 128:wof + (kc + 1) * 128],
                           hmb[kc], [n_("wt1") + str(sl_), n_("hmb")],
                           f"P{pb}", kc == 0, kc == DMC - 1, tag=tg + "ip")
                    act(szc[ec - DIC], pap(pb, 0, TM), AF.Silu, [f"P{pb}"],
                        n_("szc"), tag=tg + "ips", accum=(ec > DIC))
                for dc in range(DIC):
                    sz_st = dtb[dc]
                    szn = n_("dtb") + str(dc)
                    vcp(sz_st, szc[dc][:, ::-1], [n_("szc")], szn, tag=tg + "szr")
                    dma(dram["sz_hbm"][dc * 128:(dc + 1) * 128, mcol], sz_st,
                        "sz_hbm", rbufs=[szn], tag=tg + "ssz", accum=True,
                        chan=szn)
            else:
                for dc in range(DIC):
                    dma(szc[dc], dram["sz_hbm"][dc * 128:(dc + 1) * 128, scols],
                        n_("szc"), rbufs=["sz_hbm"], tag=tg + "lsz",
                        accum=(dc > 0))

        def s2(ch):
            tg, _, _, _ = chcols(ch)
            if ch == 0 and p == 1:
                memset(ap("carry"), 0.0, "carry", tag=tg + "c0")
            for dc in range(DIC):
                vcp(xzx.cols(dc, 0, KC - 1), ap("carry")[:, dc * 3:dc * 3 + 3],
                    ["carry"], n_("xzx"), tag=tg + "cp", accum=True)
            for dc in range(DIC):
                vcp(ap("carry")[:, dc * 3:dc * 3 + 3], xzx.cols(dc, TM, TM + KC - 1),
                    [n_("xzx")], "carry", tag=tg + "cu", accum=(dc > 0))
            for dc in range(DIC):
                cac = "mtA" if dc % 2 == 0 else "mtC"
                tsc(a_(cac), xzx.cols(dc, 0, TM),
                    ap(f"s_convw_{w}")[:, dc * KC:dc * KC + 1], OP.mult,
                    [n_("xzx"), f"s_convw_{w}"], n_(cac), tag=tg + "cv0")
                for k_ in range(1, KC):
                    stt(a_(cac), xzx.cols(dc, k_, k_ + TM),
                        ap(f"s_convw_{w}")[:, dc * KC + k_:dc * KC + k_ + 1],
                        a_(cac), OP.mult, OP.add,
                        [n_("xzx"), f"s_convw_{w}", n_(cac)], n_(cac),
                        tag=tg + f"cv{k_}")
                act(xcv[dc], a_(cac), AF.Silu, [n_(cac), f"s_convb_{w}"],
                    n_("xcv"), bias=ap(f"s_convb_{w}")[:, dc:dc + 1],
                    tag=tg + "cvs", accum=(dc > 0))
                act(xcv16[dc], a_(cac), AF.Silu, [n_(cac), f"s_convb_{w}"],
                    n_("xcv16"), bias=ap(f"s_convb_{w}")[:, dc:dc + 1],
                    tag=tg + "cvc", accum=(dc > 0))
            for dc in range(DIC):
                mm(pap(7, 0, TM, 0, RK + 2 * NST), WxT16[w][dc], xcv16[dc],
                   [f"WxT16_{w}", n_("xcv16")], "P7", dc == 0, dc == DIC - 1,
                   tag=tg + "xd")
            scop(a_("xdbl")[:, 0:TM], pap(7, 0, TM, 0, RK + 2 * NST), ["P7"],
                 n_("xdbl"), tag=tg + "xde")
            vcp(a_("xdbl16")[:, 0:TM], a_("xdbl")[0:RK, 0:TM], [n_("xdbl")],
                n_("xdbl16"), tag=tg + "xdc")
            for grp in range(4):
                ns0 = grp * 4
                dma(a_("bcr")[0:1, 0:4 * TM],
                    bass.AP(sb[n_("xdbl")], (RK + ns0) * TM, [[TM, 4], [1, TM]]),
                    n_("bcr"), rbufs=[n_("xdbl")], tag=tg + "bcB",
                    accum=(grp > 0))
                dma(a_("bcr")[0:1, 4 * TM:8 * TM],
                    bass.AP(sb[n_("xdbl")], (RK + NST + ns0) * TM,
                            [[TM, 4], [1, TM]]),
                    n_("bcr"), rbufs=[n_("xdbl")], tag=tg + "bcC",
                    accum=True)
                for half in range(2):
                    pb_ = (4, 0)[half]
                    pc_ = (5, 1)[half]
                    c0 = half * 2 * TM
                    mm(pap(pb_, 0, 2 * TM), ap("ones1"),
                       a_("bcr")[0:1, c0:c0 + 2 * TM], ["ones1", n_("bcr")],
                       f"P{pb_}", True, True, tag=tg + "bb")
                    mm(pap(pc_, 0, 2 * TM), ap("ones1"),
                       a_("bcr")[0:1, c0 + 4 * TM:c0 + 6 * TM],
                       ["ones1", n_("bcr")], f"P{pc_}", True, True,
                       tag=tg + "cb")
                    nsq = ns0 + 2 * half
                    scop(Ball.cols(nsq, 0, 2 * TM), pap(pb_, 0, 2 * TM),
                         [f"P{pb_}"], n_("Ball"), tag=tg + "be",
                         accum=(grp + half > 0))
                    scop(Call.cols(nsq, 0, 2 * TM), pap(pc_, 0, 2 * TM),
                         [f"P{pc_}"], n_("Call"), tag=tg + "ce",
                         accum=(grp + half > 0))
            for dcg in range(8):
                g0 = dcg * 2
                dnames = [n_("dtb") + str(g0 + q) for q in range(2)]
                xnames = [n_("dA2") + "0", n_("dA2") + "1"]
                for q in range(2):
                    dc = g0 + q
                    pd = 3 if dc % 2 == 0 else 6
                    mm(pap(pd, 0, TM, 0, 128),
                       ap(f"WdtT16_{w}")[:, dc * 128:(dc + 1) * 128],
                       a_("xdbl16")[:, 0:TM], [f"WdtT16_{w}", n_("xdbl16")],
                       f"P{pd}", True, True, tag=tg + "dt")
                    act(dtb[dc], pap(pd, 0, TM, 0, 128), AF.Identity,
                        [f"P{pd}", f"s_bdt_{w}"], dnames[q],
                        bias=ap(f"s_bdt_{w}")[:, dc:dc + 1], tag=tg + "dte")
                gsl = dtb.t.ap()[:, g0 * TM:(g0 + 2) * TM]
                ir.op("vector",
                      lambda e, o=a_("dA2"), i=gsl: e.tensor_scalar(
                          out=o, in0=i, scalar1=0.0, scalar2=None, op0=OP.min),
                      reads=[B(nm) for nm in dnames],
                      writes=[B(nm) for nm in xnames], tag=tg + "spm")
                ir.op("scalar",
                      lambda e, o=a_("dA2"): e.activation(out=o, in_=o,
                                                          func=AF.Exp),
                      reads=[B(nm) for nm in xnames],
                      writes=[B(nm) for nm in xnames], tag=tg + "spe")
                ir.op("scalar",
                      lambda e, o=a_("dA2"): e.activation(out=o, in_=o,
                                                          func=AF.Ln,
                                                          bias=1.0, scale=1.0),
                      reads=[B(nm) for nm in xnames],
                      writes=[B(nm) for nm in xnames], tag=tg + "spl")
                ir.op("vector",
                      lambda e, o=gsl: e.tensor_scalar(
                          out=o, in0=o, scalar1=0.0, scalar2=None, op0=OP.max),
                      reads=[B(nm) for nm in dnames],
                      writes=[B(nm) for nm in dnames], tag=tg + "spx")
                ir.op("vector",
                      lambda e, o=gsl, x=a_("dA2"): e.tensor_tensor(
                          out=o, in0=o, in1=x, op=OP.add),
                      reads=[B(nm) for nm in dnames + xnames],
                      writes=[B(nm) for nm in dnames], tag=tg + "spa")

        def s3(ch, inject):
            tg, _, _, _ = chcols(ch)
            for dc in range(DIC):
                tt(a_("dtxs"), dtb[dc], xcv[dc], OP.mult,
                   [n_("dtb") + str(dc), n_("xcv")], n_("dtxs"), tag=tg + "dtx",
                   engine="gpsimd")
                yb = 2 if dc % 2 == 0 else 7
                for ns in range(NST):
                    h_ = (ns % 2) * TM
                    hb = str(ns % 2)
                    act(a_("dA2")[:, h_:h_ + TM], dtb[dc], AF.Exp,
                        [n_("dtb") + str(dc), f"s_A_{w}"], n_("dA2") + hb,
                        scale=ap(f"s_A_{w}")[:, dc * NST + ns:dc * NST + ns + 1],
                        tag=tg + "dA")
                    tt(a_("dB2")[:, h_:h_ + TM], a_("dtxs"), Ball[ns],
                       OP.mult, [n_("dtxs"), n_("Ball")], n_("dB2") + hb,
                       tag=tg + "dB", engine="gpsimd")
                    dcn = dc * NST + ns
                    vgen(lambda e, hh_=h_, q=dcn: e.tensor_tensor_scan(
                             out=a_("hc2")[:, hh_:hh_ + TM],
                             data0=a_("dA2")[:, hh_:hh_ + TM],
                             data1=a_("dB2")[:, hh_:hh_ + TM],
                             initial=ap("state")[:, q:q + 1],
                             op0=OP.mult, op1=OP.add),
                         [n_("dA2") + hb, n_("dB2") + hb, "state"],
                         n_("hc2") + hb, tag=tg + "sc")
                    if ns % 2 == 1:
                        vcp(ap("state")[:, dcn - 1:dcn + 1],
                            bass.AP(sb[n_("hc2")], TM - 1,
                                    [[2 * TM, 128], [TM, 2]]),
                            [n_("hc2") + "0", n_("hc2") + "1"], "state",
                            tag=tg + "su", accum=True)
                    tt(a_("gq2")[:, h_:h_ + TM], a_("hc2")[:, h_:h_ + TM],
                       Call[ns], OP.mult, [n_("hc2") + hb, n_("Call")],
                       n_("gq2") + hb, tag=tg + "g", engine="gpsimd")
                    mm(pap(yb, 0, TM), ap("ident16"), a_("gq2")[:, h_:h_ + TM],
                       ["ident16", n_("gq2") + hb], f"P{yb}", ns == 0,
                       ns == NST - 1, tag=tg + "ga")
                ydt = "mtB" if dc % 2 == 0 else "mtD"
                stt(a_(ydt), xcv[dc], ap(f"s_Dp_{w}")[:, dc:dc + 1],
                    pap(yb, 0, TM), OP.mult, OP.add,
                    [n_("xcv"), f"s_Dp_{w}", f"P{yb}"], n_(ydt), tag=tg + "yd")
                tt(gated[dc], a_(ydt), szc[dc], OP.mult, [n_(ydt), n_("szc")],
                   n_("gated"), tag=tg + "gt", accum=(dc > 0))
                inject(dc)

        def s4(ch):
            tg, tcols, scols, mcol = chcols(ch)
            for mc in range(DMC):
                sl_ = mc % 2
                wof = sl_ * DIC * 128
                po = 6 if mc % 2 == 0 else 5
                dma(a_("wt2")[:, wof:wof + DIC * 128],
                    bass.AP(dram["WoutT16"].tensor, mc * 128,
                            [[DM, 128], [128 * DM, DIC], [1, 128]]),
                    n_("wt2") + str(sl_), rbufs=["WoutT16"], tag=tg + "opw")
                for dc in range(DIC):
                    mm(pap(po, 0, TM),
                       a_("wt2")[:, wof + dc * 128:wof + (dc + 1) * 128],
                       gated[dc], [n_("wt2") + str(sl_), n_("gated")],
                       f"P{po}", dc == 0, dc == DIC - 1, tag=tg + "op")
                if p == 1:
                    scop(oev[mc], pap(po, 0, TM), [f"P{po}"],
                         n_("oev") + str(mc), tag=tg + "ope")
                    dma(dram["o1_hbm"][mc * 128:(mc + 1) * 128, tcols], oev[mc],
                        "o1_hbm", rbufs=[n_("oev") + str(mc)], tag=tg + "so1",
                        accum=True, chan=n_("oev") + str(mc))
                else:
                    dma(x2r[sl_], dram["x2_hbm"][mc * 128:(mc + 1) * 128, tcols],
                        n_("x2r") + str(sl_), rbufs=["x2_hbm"], tag=tg + "lx2r")
                    dma(o1r[sl_], dram["o1_hbm"][mc * 128:(mc + 1) * 128, tcols],
                        n_("o1r") + str(sl_), rbufs=["o1_hbm"],
                        tag=tg + "lo1r")
                    tt(oev[mc], x2r[sl_][:, ::-1], o1r[sl_][:, ::-1], OP.add,
                       [n_("x2r") + str(sl_), n_("o1r") + str(sl_)],
                       n_("oev") + str(mc), tag=tg + "x3a")
                    tt(oev[mc], oev[mc], pap(po, 0, TM), OP.add,
                       [n_("oev") + str(mc), f"P{po}"], n_("oev") + str(mc),
                       tag=tg + "x3b", accum=True)
                    dma(dram["x3_hbm"][mc * 128:(mc + 1) * 128, scols], oev[mc],
                        "x3_hbm", rbufs=[n_("oev") + str(mc)], tag=tg + "sx3",
                        accum=True, chan=n_("oev") + str(mc))

        def inj_mk(nxt):
            def inj(dc):
                if nxt >= NMC:
                    return
                if dc == 0:
                    s0(nxt)
                elif 1 <= dc <= 8:
                    s1x(nxt, [2 * (dc - 1), 2 * (dc - 1) + 1])
            return inj

        s0(0)
        s1x(0, list(range(DIC)))
        s1z(0)
        s2(0)
        for ch in range(NMC):
            s3(ch, inj_mk(ch + 1))
            s4(ch)
            if ch + 1 < NMC:
                s1z(ch + 1)
                s2(ch + 1)
        if p == 1:
            vcp(ap("agin")[:, 0:DIC * NST], ap("state"), ["state"], "ago0",
                tag="agst")
            for dc in range(DIC):
                vcp(ap("agin")[:, DIC * NST + dc * 3:DIC * NST + dc * 3 + 3],
                    xzx.cols(dc, TM, TM + KC - 1), [n_("xzx")], "ago0", tag="aghl",
                    accum=True)
            dma(dram["cin"], ap("agin"), "cin", rbufs=["ago0"], tag="agci")
            ir.op("gpsimd",
                  lambda e: e.collective_compute(
                      "AllGather", OP.bypass,
                      replica_groups=[[0, 1], [2, 3], [4, 5], [6, 7]],
                      ins=[dram["cin"].opt()], outs=[dram["cout"].opt()]),
                  reads=[B("cin")], writes=[B("cout")], tag="ag")
            dma(ap("ago0"), dram["cout"][0:128, :], "ago0", rbufs=["cout"], tag="ag0")
            dma(ap("ago1"), dram["cout"][128:256, :], "ago1", rbufs=["cout"],
                tag="ag1")
            tsc(ap("ago1"), ap("ago1"), ap("selm")[:, 1:2], OP.mult,
                ["ago1", "selm"], "ago1", tag="agm1")
            stt(ap("ago1"), ap("ago0"), ap("selm")[:, 0:1], ap("ago1"), OP.mult,
                OP.add, ["ago0", "selm", "ago1"], "ago1", tag="agm0")
            vcp(ap("state"), ap("ago1")[:, 0:DIC * NST], ["ago1"], "state",
                tag="ags2")
            for dc in range(DIC):
                src = ap("ago1")[:, DIC * NST + dc * 3:DIC * NST + dc * 3 + 3]
                vcp(ap("carry")[:, dc * 3:dc * 3 + 3], src, ["ago1"],
                    "carry", tag="agc2", accum=(dc > 0))
        mstack.close()

    if phases & 1:
        emit_ffn("xloc", "x2_hbm", 0)
        ir.barrier("b0")
    if phases & 2:
        emit_mamba(1)
        ir.barrier("b1")
    if phases & 4:
        emit_mamba(2)
        ir.barrier("b2")
    if phases & 8:
        emit_ffn("x3_hbm", "out", 1)
    else:
        srcn = "x3_hbm" if (phases & 4) else "x2_hbm"
        sbuf("copybb", [128, OWN], F32)
        for mc in range(DMC):
            dma(sb["copybb"].ap(), dram[srcn][mc * 128:(mc + 1) * 128, :], "copybb",
                rbufs=[srcn], tag=f"cb{mc}")
            dma(dram["out"][mc * 128:(mc + 1) * 128, :], sb["copybb"].ap(), "out",
                rbufs=["copybb"], tag=f"cbo{mc}", accum=(mc > 0))
    ir.op("gpsimd", lambda e: e.nop(), reads=[B("out")], tag="final")

    _build_block(nc, ir)
    gstack.close()
    return nc, ir



def _prep_core_inputs(inp, OWN):
    f32 = np.float32
    x = np.asarray(inp["L0_attn_hidden_states"], f32)
    W_in = np.asarray(inp["L0_attn_W_in"], f32)
    W_out = np.asarray(inp["L0_attn_W_out"], f32)

    def pack_col(a, nch):
        return np.ascontiguousarray(np.asarray(a, f32).reshape(nch, 128).T)

    def pack_cols(a, nch):
        a = np.asarray(a, f32)
        w = a.shape[1]
        return np.ascontiguousarray(a.reshape(nch, 128, w).transpose(1, 0, 2)
                                    .reshape(128, nch * w))

    ew2 = np.asarray(inp["L0_ffn_expert_w2"], f32)
    ew2mc = np.empty((DMC * HH, NE * 3 * 128), f32)
    for mc in range(DMC):
        for e_ in range(NE):
            for part in range(3):
                ew2mc[mc * HH:(mc + 1) * HH,
                      e_ * 384 + part * 128:e_ * 384 + (part + 1) * 128] = \
                    ew2[e_, :, part * DM + mc * 128: part * DM + (mc + 1) * 128]

    shared = {
        "rw1": np.ascontiguousarray(inp["L0_ffn_router_w1"], f32),
        "rb1": pack_col(inp["L0_ffn_router_b1"], 2),
        "rw2": np.ascontiguousarray(inp["L0_ffn_router_w2"], f32),
        "rb2": np.asarray(inp["L0_ffn_router_b2"], f32).reshape(NE, 1),
        "hw1": np.ascontiguousarray(inp["L0_ffn_halt_w1"], f32),
        "hb1": pack_col(inp["L0_ffn_halt_b1"], 2),
        "hw2": np.ascontiguousarray(inp["L0_ffn_halt_w2"], f32),
        "hb2": np.asarray(inp["L0_ffn_halt_b2"], f32).reshape(1, 1),
        "shw1": np.ascontiguousarray(inp["L0_ffn_shared_w1"], f32),
        "shb1": np.asarray(inp["L0_ffn_shared_b1"], f32).reshape(HH, 1),
        "shw2": np.ascontiguousarray(inp["L0_ffn_shared_w2"], f32),
        "shb2": pack_col(inp["L0_ffn_shared_b2"], 24),
        "ew1": np.ascontiguousarray(inp["L0_ffn_expert_w1"], f32),
        "eb1": np.ascontiguousarray(np.asarray(inp["L0_ffn_expert_b1"], f32).T),
        "ew2mc": ew2mc,
        "eb2": np.ascontiguousarray(
            np.asarray(inp["L0_ffn_expert_b2"], f32).reshape(NE, 24, 128)
            .transpose(2, 0, 1).reshape(128, NE * 24)),
        "rfw1": np.ascontiguousarray(inp["L0_ffn_refine_w1"], f32),
        "rfb1": pack_col(inp["L0_ffn_refine_b1"], 16),
        "rfw2": np.ascontiguousarray(inp["L0_ffn_refine_w2"], f32),
        "rfb2": pack_col(inp["L0_ffn_refine_b2"], 8),
        "swt": np.asarray(inp["L0_ffn_shared_weight"], f32).reshape(1, 1),
        "mu": pack_col(inp["L0_ffn_mu"], 8),
        "iw": pack_col(inp["L0_ffn_integration_weight"], 8),
        "WinT": np.ascontiguousarray(W_in.T),
        "WoutT": np.ascontiguousarray(W_out.T),
    }
    dirsets = {}
    for d in ("fwd", "rev"):
        dirsets[d] = {
            "WxT": np.ascontiguousarray(np.asarray(inp[f"L0_attn_Wx_{d}"], f32).T),
            "WdtT": np.ascontiguousarray(np.asarray(inp[f"L0_attn_Wdt_{d}"], f32).T),
            "convw": pack_cols(inp[f"L0_attn_conv_w_{d}"], DIC),
            "convb": pack_col(inp[f"L0_attn_conv_b_{d}"], DIC),
            "bdt": pack_col(inp[f"L0_attn_bdt_{d}"], DIC),
            "Alog": pack_cols(inp[f"L0_attn_A_log_{d}"], DIC),
            "Dp": pack_col(inp[f"L0_attn_D_{d}"], DIC),
        }
    in_maps = []
    for c in range(8):
        b_, h_ = c // 2, c % 2
        if h_ == 0:
            xs = x[b_, 0:OWN, :]
        else:
            xs = x[b_, 2 * OWN - 1:None if OWN == 0 else OWN - 1:-1, :] \
                if OWN > 1 else x[b_, OWN:2 * OWN, :][::-1]
            xs = x[b_, OWN:2 * OWN, :][::-1]
        m = dict(shared)
        m["xloc"] = np.ascontiguousarray(xs.T)
        sel = np.zeros((128, 2), f32)
        sel[:, 1 - h_] = 1.0
        m["selmask"] = sel
        p1d, p2d = ("fwd", "rev") if h_ == 0 else ("rev", "fwd")
        for p, d in (("p1", p1d), ("p2", p2d)):
            for kk, vv in dirsets[d].items():
                m[f"{kk}_{p}"] = vv
        in_maps.append(m)
    return in_maps


def kernel(**inputs):
    L = inputs["L0_attn_hidden_states"].shape[1]
    OWN = L // 2
    TF = 256 if OWN % 256 == 0 else 128
    TM = 256 if OWN % 256 == 0 else 128
    nc, _ = build_program(OWN, TF, TM)
    in_maps = _prep_core_inputs(inputs, OWN)
    res = run_bass_kernel_spmd(nc, in_maps, core_ids=list(range(8)))
    Bsz = inputs["L0_attn_hidden_states"].shape[0]
    out = np.empty((Bsz, L, DM), np.float32)
    for c in range(8):
        b_, h_ = c // 2, c % 2
        o = np.asarray(res.results[c]["out"])
        if h_ == 0:
            out[b_, 0:OWN, :] = o.T[::-1]
        else:
            out[b_, OWN:2 * OWN, :] = o.T
    return out


# version 17 (final)
# speedup vs baseline: 1.0906x; 1.0906x over previous
import contextlib
from collections import defaultdict

import numpy as np
import ml_dtypes

import concourse.bass as bass
import concourse.mybir as mybir
from concourse.bass_utils import run_bass_kernel_spmd

F32 = mybir.dt.float32
BF16 = mybir.dt.bfloat16
U16 = mybir.dt.uint16
U32 = mybir.dt.uint32
AF = mybir.ActivationFunctionType
OP = mybir.AluOpType
BF = ml_dtypes.bfloat16



class _Buf:
    __slots__ = ("name", "writers", "readers")

    def __init__(self, name):
        self.name = name
        self.writers = []
        self.readers = []


class _Node:
    __slots__ = ("idx", "engine", "emit", "deps", "dma", "channel",
                 "inc_needed", "sem_after", "tag")

    def __init__(self, idx, engine, emit, dma, channel, tag):
        self.idx = idx
        self.engine = engine
        self.emit = emit
        self.deps = []
        self.dma = dma
        self.channel = channel
        self.inc_needed = dma
        self.sem_after = None
        self.tag = tag


class IR:
    ENGINES = ("tensor", "vector", "scalar", "gpsimd", "sync")

    def __init__(self):
        self.nodes = []
        self._bufs = {}

    def buf(self, name):
        if name not in self._bufs:
            self._bufs[name] = _Buf(name)
        return self._bufs[name]

    def op(self, engine, emit, reads=(), writes=(), writes_accum=(), dma=False,
           tag="", channel=None):
        if dma and channel is None:
            wl = list(writes) + list(writes_accum)
            assert len(wl) == 1, f"dma op must write exactly one buf: {tag}"
            channel = wl[0].name
        n = _Node(len(self.nodes), engine, emit, dma, channel, tag)
        dep_nodes = set()
        for x in reads:
            dep_nodes.update(x.writers)
        for x in writes:
            dep_nodes.update(x.writers)
            dep_nodes.update(x.readers)
        for x in writes_accum:
            dep_nodes.update(x.readers)
        for d in sorted(dep_nodes, key=lambda q: q.idx):
            n.deps.append(d)
            d.inc_needed = True
        self.nodes.append(n)
        for x in reads:
            x.readers.append(n)
        for x in writes:
            x.writers = [n]
            x.readers = []
        for x in writes_accum:
            x.writers.append(n)
        return n

    def barrier(self, tag="bar"):
        live = [x for x in self._bufs.values() if x.writers or x.readers]
        bb = self.buf(f"_{tag}_{len(self.nodes)}")
        gn = self.op("gpsimd", lambda e: e.nop(), reads=live, writes=[bb],
                     tag=tag + "_g")
        ens = []
        for eng in ("tensor", "vector", "scalar", "sync"):
            ens.append(self.op(eng, lambda e: e.nop(), reads=[bb],
                               tag=tag + "_" + eng))
        for x in live:
            x.writers = [gn] + ens
            x.readers = []

    def finalize(self):
        eng_c = defaultdict(int)
        chan_c = defaultdict(int)
        for n in self.nodes:
            if n.dma:
                chan_c[n.channel] += 16
                n.sem_after = chan_c[n.channel]
            elif n.inc_needed:
                eng_c[n.engine] += 1
                n.sem_after = eng_c[n.engine]
        self.per_engine = {e: [n for n in self.nodes if n.engine == e]
                           for e in self.ENGINES}
        self.channels = sorted(chan_c)
        return self

    def sem_key(self, n):
        return ("chan", n.channel) if n.dma else ("eng", n.engine)

    def emit_engine(self, engine, eng, sems):
        waited = defaultdict(int)
        for n in self.per_engine[engine]:
            need = {}
            for d in n.deps:
                k = self.sem_key(d)
                if d.sem_after > need.get(k, 0):
                    need[k] = d.sem_after
            for k, v in sorted(need.items(), key=lambda kv: str(kv[0])):
                if v > waited[k]:
                    eng.wait_ge(sems[k], v)
                    waited[k] = v
            inst = n.emit(eng)
            if n.inc_needed:
                assert inst is not None, f"node {n.idx} ({n.tag}) needs inc, emit=None"
                inst.then_inc(sems[self.sem_key(n)], 16 if n.dma else 1)


def _build_block(nc, ir):
    ir.finalize()
    st = contextlib.ExitStack()
    sems = {}
    for e in IR.ENGINES:
        sems[("eng", e)] = st.enter_context(nc.semaphore(f"se_{e}"))
    for i, ch in enumerate(ir.channels):
        sems[("chan", ch)] = st.enter_context(nc.semaphore(f"sd{i}"))
    block = st.enter_context(nc.Block())

    @block.tensor
    def _(eng):
        ir.emit_engine("tensor", eng, sems)

    @block.vector
    def _(eng):
        ir.emit_engine("vector", eng, sems)

    @block.scalar
    def _(eng):
        ir.emit_engine("scalar", eng, sems)

    @block.gpsimd
    def _(eng):
        ir.emit_engine("gpsimd", eng, sems)

    @block.sync
    def _(eng):
        ir.emit_engine("sync", eng, sems)

    st.close()



DM, DI, NST, RK, KC, NE, HH, DH = 1024, 2048, 16, 64, 4, 8, 64, 256
DMC, DIC = DM // 128, DI // 128
DT_CONST = 0.1
FFN_UPTO = 99
FFN_SK = set()


def build_program(OWN, TF, TM, dbg=False, phases=15):
    NFT, NMC, SUB = OWN // TF, OWN // TM, TF // 128
    nc = bass.Bass("TRN2", target_bir_lowering=False)
    ir = IR()
    gstack = contextlib.ExitStack()
    B = ir.buf
    dram = {}
    sb = {}

    def din(name, shape, dt=F32):
        dram[name] = nc.dram_tensor(name, list(shape), dt, kind="ExternalInput").ap()

    def dout(name, shape, dt=F32):
        dram[name] = nc.dram_tensor(name, list(shape), dt, kind="ExternalOutput").ap()

    def dscr(name, shape, dt=F32):
        if dbg:
            dram[name] = nc.dram_tensor(name, list(shape), dt,
                                        kind="ExternalOutput").ap()
        else:
            dram[name] = nc.dram_tensor(name, list(shape), dt).ap()

    def sbuf(name, shape, dt=F32, stack=None):
        sb[name] = (stack or gstack).enter_context(nc.sbuf_tensor(name, shape, dt))
        return sb[name]

    class MTT:
        def __init__(self, name, nch, width, dt=F32, prow=128, stack=None):
            self.t = sbuf(name, [prow, nch * width], dt, stack)
            self.w = width
            self.name = name

        def __getitem__(self, ch):
            return self.t.ap()[:, ch * self.w:(ch + 1) * self.w]

        def cols(self, ch, c0, c1):
            return self.t.ap()[:, ch * self.w + c0: ch * self.w + c1]

        def ap(self):
            return self.t.ap()

    def ap(n):
        return sb[n].ap()

    din("xloc", [DM, OWN])
    din("selmask", [128, 2])
    din("rw1", [DM, DH]); din("rb1", [128, 2])
    din("rw2", [DH, NE]); din("rb2", [NE, 1])
    din("hw1", [DM, DH]); din("hb1", [128, 2])
    din("hw2", [DH, 1]); din("hb2", [1, 1])
    din("shw1", [2 * DM, HH]); din("shb1", [HH, 1])
    din("shw2", [HH, 3 * DM]); din("shb2", [128, 24])
    din("ew1", [NE, 2 * DM, HH]); din("eb1", [HH, NE])
    din("ew2mc", [DMC * HH, NE * 3 * 128])
    din("eb2", [128, NE * 24])
    din("rfw1", [DM, 2 * DM]); din("rfb1", [128, 16])
    din("rfw2", [2 * DM, DM]); din("rfb2", [128, 8])
    din("swt", [1, 1]); din("mu", [128, 8]); din("iw", [128, 8])
    din("WinT", [DM, 2 * DI]); din("WoutT", [DI, DM])
    for p in ("p1", "p2"):
        din(f"WxT_{p}", [DI, RK + 2 * NST])
        din(f"WdtT_{p}", [RK, DI])
        din(f"convw_{p}", [128, DIC * KC])
        din(f"convb_{p}", [128, DIC])
        din(f"bdt_{p}", [128, DIC])
        din(f"Alog_{p}", [128, DIC * NST])
        din(f"Dp_{p}", [128, DIC])
    dout("out", [DM, OWN])
    dscr("x2_hbm", [DM, OWN])
    dscr("dbgI", [DM, OWN]); dscr("dbgV", [DM, OWN])
    dscr("dbgSv", [DM, OWN]); dscr("dbgSx", [DM, OWN])
    dscr("dbgXsh", [DM, OWN]); dscr("dbgVsh", [DM, OWN])
    dscr("dbgHlt", [1, OWN]); dscr("dbgW", [128, OWN // 128 * 2])
    dscr("dbgSelA", [128, 2 * OWN])
    dscr("h_hbm", [DM, OWN], BF16)
    dscr("sz_hbm", [DI, OWN])
    dscr("o1_hbm", [DM, OWN])
    dscr("x3_hbm", [DM, OWN])
    dram["WinT16"] = nc.dram_tensor("WinT16", [DM, 2 * DI], BF16).ap()
    dram["WoutT16"] = nc.dram_tensor("WoutT16", [DI, DM], BF16).ap()
    dram["rfw1_16"] = nc.dram_tensor("rfw1_16", [DM, 2 * DM], BF16).ap()
    dram["rfw2_16"] = nc.dram_tensor("rfw2_16", [2 * DM, DM], BF16).ap()
    dram["ew2_16"] = nc.dram_tensor("ew2_16", [DMC * HH, NE * 3 * 128], BF16).ap()
    AGW = DIC * NST + DIC * 3
    dram["cin"] = nc.dram_tensor("cin", [128, AGW], F32).ap()
    dram["cout"] = nc.dram_tensor("cout", [256, AGW], F32).ap()

    sbuf("ident", [128, 128], F32)
    sbuf("ident16", [128, 128], BF16)
    sbuf("ones1", [1, 128], F32)
    sbuf("onesc", [128, 1], F32)
    sbuf("eps1", [1, 1], F32)
    sbuf("mhalf", [1, 1], F32)
    sbuf("iotaS", [128, SUB], F32)
    rw1_16 = MTT("rw1_16", DMC, DH, BF16)
    sbuf("rw2_16", [128, 2 * NE], BF16)
    hw1_16 = MTT("hw1_16", DMC, DH, BF16)
    sbuf("hw2f", [128, 2], F32)
    shw1_16 = MTT("shw1_16", 16, HH, BF16)
    sbuf("shw2_16", [HH, 3 * DM], BF16)
    ew1_16 = MTT("ew1_16", NE * 16, HH, BF16)
    WxT16 = {p: MTT(f"WxT16_{p}", DIC, RK + 2 * NST, BF16) for p in ("p1", "p2")}
    for p in ("p1", "p2"):
        sbuf(f"WdtT16_{p}", [RK, DI], BF16)
    for name, shape in [("rb1s", [128, 2]), ("rb2s", [NE, 1]), ("hb1s", [128, 2]),
                        ("hb2s", [1, 1]), ("shb1s", [HH, 1]), ("shb2s", [128, 24]),
                        ("eb1s", [HH, NE]), ("eb2s", [128, NE * 24]),
                        ("rfb1s", [128, 16]), ("rfb2s", [128, 8]),
                        ("mus", [128, 8]), ("iws", [128, 8]),
                        ("swsc", [1, 1]), ("swcol", [128, 1]), ("onemsw", [128, 1]),
                        ("selm", [128, 2])]:
        sbuf(name, shape, F32)
    for p in ("p1", "p2"):
        for name, w in [("convw", DIC * KC), ("convb", DIC), ("bdt", DIC),
                        ("A", DIC * NST), ("Dp", DIC)]:
            sbuf(f"s_{name}_{p}", [128, w], F32)
    sbuf("state", [128, DIC * NST], F32)
    sbuf("carry", [128, DIC * 3], F32)
    sbuf("ago0", [128, AGW], F32)
    sbuf("ago1", [128, AGW], F32)
    sb["agin"] = sb["ago0"]
    for i in range(8):
        sb[f"P{i}"] = gstack.enter_context(nc.psum_tensor(f"P{i}", [128, 512], F32))

    def pap(i, c0=0, c1=512, p0=0, p1=128):
        return sb[f"P{i}"].ap()[p0:p1, c0:c1]

    def dma(dst, src, wbuf, rbufs=(), tag="dma", accum=False, slow=False, chan=None):
        kw = dict(writes_accum=[B(wbuf)]) if accum else dict(writes=[B(wbuf)])
        ir.op("sync", lambda e, d=dst, s=src, sl=slow:
                  e.dma_start(out=d, in_=s, allow_slow_non_contiguous=True) if sl
                  else e.dma_start(out=d, in_=s),
              reads=[B(r) for r in rbufs], dma=True, tag=tag, channel=chan, **kw)

    def mm(out, lhsT, rhs, rbufs, wbuf, start, stop, tag="mm"):
        if start:
            kw = dict(writes=[B(wbuf)])
        elif stop:
            kw = dict(writes_accum=[B(wbuf)])
        else:
            kw = {}
        ir.op("tensor",
              lambda e, o=out, l=lhsT, r=rhs, s0=start, s1=stop:
                  e.matmul(o, l, r, start=s0, stop=s1),
              reads=[B(r) for r in rbufs], tag=tag, **kw)

    def trans(out, in_, rbufs, wbuf, np_=128, tag="tr"):
        ir.op("tensor",
              lambda e, o=out, i=in_, d=ap("ident")[0:np_, 0:np_]: e.transpose(o, i, d),
              reads=[B(r) for r in rbufs] + [B("ident")], writes=[B(wbuf)], tag=tag)

    def act(out, in_, func, rbufs, wbuf, bias=None, scale=None, tag="act", accum=False):
        kw = {}
        if bias is not None:
            kw["bias"] = bias
        if scale is not None:
            kw["scale"] = scale
        wkw = dict(writes_accum=[B(wbuf)]) if accum else dict(writes=[B(wbuf)])
        ir.op("scalar",
              lambda e, o=out, i=in_, f=func, k=dict(kw):
                  e.activation(out=o, in_=i, func=f, **k),
              reads=[B(r) for r in rbufs], tag=tag, **wkw)

    def scop(out, in_, rbufs, wbuf, tag="scop", accum=False):
        wkw = dict(writes_accum=[B(wbuf)]) if accum else dict(writes=[B(wbuf)])
        ir.op("scalar", lambda e, o=out, i=in_: e.copy(out=o, in_=i),
              reads=[B(r) for r in rbufs], tag=tag, **wkw)

    def vgen(emit, rbufs, wbuf, tag="v", engine="vector", accum=False):
        wkw = dict(writes_accum=[B(wbuf)]) if accum else dict(writes=[B(wbuf)])
        ir.op(engine, emit, reads=[B(r) for r in rbufs], tag=tag, **wkw)

    def tt(out, a, bb, op, rbufs, wbuf, tag="tt", engine="vector", accum=False):
        vgen(lambda e, o=out, x=a, y=bb, p=op: e.tensor_tensor(out=o, in0=x, in1=y, op=p),
             rbufs, wbuf, tag, engine, accum)

    def tsc(out, in_, s1, op0, rbufs, wbuf, s2=None, op1=None, tag="ts", accum=False,
            engine="vector"):
        def em(e, o=out, i=in_, a=s1, b2=s2, p0=op0, p1=op1):
            if p1 is not None:
                return e.tensor_scalar(out=o, in0=i, scalar1=a, scalar2=b2,
                                       op0=p0, op1=p1)
            return e.tensor_scalar(out=o, in0=i, scalar1=a, scalar2=None, op0=p0)
        vgen(em, rbufs, wbuf, tag, engine, accum)

    def stt(out, in0, scalar, in1, op0, op1, rbufs, wbuf, tag="stt", accum=False,
            engine="vector"):
        vgen(lambda e, o=out, a=in0, s=scalar, b2=in1, p0=op0, p1=op1:
                 e.scalar_tensor_tensor(out=o, in0=a, scalar=s, in1=b2, op0=p0, op1=p1),
             rbufs, wbuf, tag, engine, accum)

    def vcp(out, in_, rbufs, wbuf, tag="vcp", accum=False, engine="vector"):
        vgen(lambda e, o=out, i=in_: e.tensor_copy(o, i), rbufs, wbuf, tag,
             engine, accum)

    def softplus(out, x_ap, bias_col, rbufs, wbuf, t1, t2, n1, n2, tag="sp",
                 engine="vector", accum=False):
        act(t2, x_ap, AF.Relu, list(rbufs), n2, bias=bias_col, tag=tag + "r")
        stt(out, x_ap, bias_col, t2, OP.add, OP.subtract, list(rbufs) + [n2],
            wbuf, tag=tag + "m", accum=accum)
        act(out, out, AF.Exp, [wbuf], wbuf, tag=tag + "e", accum=True)
        act(out, out, AF.Ln, [wbuf], wbuf, bias=1.0, scale=1.0, tag=tag + "l",
            accum=True)
        tt(out, out, t2, OP.add, [wbuf, n2], wbuf, tag=tag + "a", engine=engine,
           accum=True)

    def memset(apx, val, wbuf, tag="ms", accum=False):
        vgen(lambda e, o=apx, v=val: e.memset(o, v), [], wbuf, tag, "vector", accum)

    pstack = contextlib.ExitStack()
    sbuf("stage", [128, 3072], F32, stack=pstack)
    sbuf("cst", [128, 3072], BF16, stack=pstack)
    ir.op("gpsimd", lambda e: e.iota(ap("ident"), [[1, 128]], channel_multiplier=-1,
                                     base=0, allow_small_or_imprecise_dtypes=True),
          writes=[B("ident")], tag="iota_id")
    tsc(ap("ident"), ap("ident"), 0.0, OP.is_equal, ["ident"], "ident", tag="id_eq")
    vcp(ap("ident16"), ap("ident"), ["ident"], "ident16", tag="id16")
    memset(ap("ones1"), 1.0, "ones1")
    memset(ap("onesc"), 1.0, "onesc")
    memset(ap("eps1"), 1e-5, "eps1")
    memset(ap("mhalf"), -0.5, "mhalf")
    ir.op("gpsimd", lambda e: e.iota(ap("iotaS"), [[128, SUB]], channel_multiplier=1,
                                     base=0, allow_small_or_imprecise_dtypes=True),
          writes=[B("iotaS")], tag="iota_s")

    for sn, dn in [("rb1s", "rb1"), ("rb2s", "rb2"), ("hb1s", "hb1"), ("hb2s", "hb2"),
                   ("shb1s", "shb1"), ("shb2s", "shb2"), ("eb1s", "eb1"),
                   ("eb2s", "eb2"), ("rfb1s", "rfb1"), ("rfb2s", "rfb2"),
                   ("mus", "mu"), ("iws", "iw"), ("selm", "selmask"),
                   ("swsc", "swt")]:
        dma(ap(sn), dram[dn], sn, tag=f"ld_{sn}", chan="ldp")
    for p in ("p1", "p2"):
        for nm in ("convw", "convb", "bdt", "Dp"):
            dma(ap(f"s_{nm}_{p}"), dram[f"{nm}_{p}"], f"s_{nm}_{p}",
                tag=f"ld{nm}{p}", chan="ldp")
        dma(ap(f"s_A_{p}"), dram[f"Alog_{p}"], f"s_A_{p}", tag=f"ldA{p}",
            chan="ldp")

    def cast_res(src_dram, rows, cols, bufname, chunk_apfn):
        nch = max(1, rows // 128)
        prow = min(rows, 128)
        for ch in range(nch):
            sa_ = ap("stage")[0:prow, 0:cols]
            dma(sa_, src_dram[ch * prow:(ch + 1) * prow, 0:cols], "stage",
                tag=f"ldc_{bufname}{ch}")
            vcp(chunk_apfn(ch), sa_, ["stage"], bufname, tag=f"c_{bufname}{ch}",
                accum=(ch > 0))

    cast_res(dram["rw1"], DM, DH, "rw1_16", lambda ch: rw1_16[ch])
    cast_res(dram["rw2"], DH, NE, "rw2_16",
             lambda ch: ap("rw2_16")[:, ch * NE:(ch + 1) * NE])
    cast_res(dram["hw1"], DM, DH, "hw1_16", lambda ch: hw1_16[ch])
    cast_res(dram["hw2"], DH, 1, "hw2f", lambda ch: ap("hw2f")[:, ch:ch + 1])
    cast_res(dram["shw1"], 2 * DM, HH, "shw1_16", lambda ch: shw1_16[ch])
    cast_res(dram["shw2"], HH, 3 * DM, "shw2_16", lambda ch: ap("shw2_16"))
    for p in ("p1", "p2"):
        cast_res(dram[f"WxT_{p}"], DI, RK + 2 * NST, f"WxT16_{p}",
                 lambda ch, pp=p: WxT16[pp][ch])
        cast_res(dram[f"WdtT_{p}"], RK, DI, f"WdtT16_{p}",
                 lambda ch, pp=p: ap(f"WdtT16_{pp}"))
    for e_ in range(NE):
        for ch in range(16):
            sa_ = ap("stage")[:, 0:HH]
            dma(sa_, dram["ew1"][e_, ch * 128:(ch + 1) * 128, :], "stage",
                tag=f"lew1_{e_}_{ch}")
            vcp(ew1_16[e_ * 16 + ch], sa_, ["stage"], "ew1_16",
                tag=f"cew1_{e_}_{ch}", accum=(e_ + ch > 0))

    def cast_hbm(dst, src, rows, cols):
        step = min(cols, 3072)
        first = True
        for ch in range((rows + 127) // 128):
            prow = min(128, rows - ch * 128)
            for c0 in range(0, cols, step):
                w_ = min(step, cols - c0)
                sa_ = ap("stage")[0:prow, 0:w_]
                dma(sa_, dram[src][ch * 128:ch * 128 + prow, c0:c0 + w_], "stage",
                    tag=f"lh_{dst}{ch}_{c0}")
                ca = ap("cst")[0:prow, 0:w_]
                vcp(ca, sa_, ["stage"], "cst", tag=f"ch_{dst}{ch}_{c0}")
                dma(dram[dst][ch * 128:ch * 128 + prow, c0:c0 + w_], ca, dst,
                    rbufs=["cst"], tag=f"sh_{dst}{ch}_{c0}", accum=not first)
                first = False

    cast_hbm("WinT16", "WinT", DM, 2 * DI)
    cast_hbm("WoutT16", "WoutT", DI, DM)
    cast_hbm("rfw1_16", "rfw1", DM, 2 * DM)
    cast_hbm("rfw2_16", "rfw2", 2 * DM, DM)
    cast_hbm("ew2_16", "ew2mc", DMC * HH, NE * 3 * 128)
    ir.barrier("bpro")
    pstack.close()
    for p in ("p1", "p2"):
        act(ap(f"s_A_{p}"), ap(f"s_A_{p}"), AF.Exp, [f"s_A_{p}"], f"s_A_{p}",
            tag="aexp")
        tsc(ap(f"s_A_{p}"), ap(f"s_A_{p}"), -1.0, OP.mult, [f"s_A_{p}"], f"s_A_{p}",
            tag="aneg")
    act(ap("swsc"), ap("swsc"), AF.Sigmoid, ["swsc"], "swsc", tag="swsig")
    mm(pap(0, 0, 1), ap("ones1"), ap("swsc"), ["ones1", "swsc"], "P0", True, True,
       tag="swb")
    scop(ap("swcol"), pap(0, 0, 1), ["P0"], "swcol", tag="swev")
    tsc(ap("onemsw"), ap("swcol"), -1.0, OP.mult, ["swcol"], "onemsw",
        s2=1.0, op1=OP.add, tag="onemsw")

    _uid = [0]

    def emit_ln(src_mt, srcname, dst32_mt, d32name, dst16_mt, d16name, T, inplace,
                lnrow, lnrow2, lntmp):
        k = _uid[0]; _uid[0] += 1
        srcn = (lambda q: srcname[q]) if isinstance(srcname, list) else (lambda q: srcname)
        dstn = (lambda q: d32name[q]) if isinstance(d32name, list) else (lambda q: d32name)
        nspl = (T + 511) // 512
        for s in range(nspl):
            c0, c1 = s * 512, min(T, (s + 1) * 512)
            for mc2 in range(DMC):
                mm(pap(3, c0, c1, 0, 1), ap("onesc"), src_mt.cols(mc2, c0, c1),
                   ["onesc", srcn(mc2)], "P3", mc2 == 0, mc2 == DMC - 1, tag=f"lnS{k}")
            for mc2 in range(DMC):
                act(ap(lntmp)[:, 0:c1 - c0], src_mt.cols(mc2, c0, c1), AF.Square,
                    [srcn(mc2)], lntmp, tag=f"lnQ{k}")
                mm(pap(6, c0, c1, 0, 1), ap("onesc"), ap(lntmp)[:, 0:c1 - c0],
                   ["onesc", lntmp], "P6", mc2 == 0, mc2 == DMC - 1, tag=f"lnQ2{k}")
        tsc(ap(lnrow)[:, 0:T], pap(3, 0, T, 0, 1), 1.0 / DM, OP.mult, ["P3"],
            lnrow, tag=f"lnM{k}")
        tsc(ap(lnrow2)[:, 0:T], pap(6, 0, T, 0, 1), 1.0 / DM, OP.mult, ["P6"],
            lnrow2, tag=f"lnV{k}")
        tt(ap(lntmp)[0:1, 0:T], ap(lnrow)[:, 0:T], ap(lnrow)[:, 0:T], OP.mult,
           [lnrow], lntmp, tag=f"lnm2{k}")
        tt(ap(lnrow2)[:, 0:T], ap(lnrow2)[:, 0:T], ap(lntmp)[0:1, 0:T],
           OP.subtract, [lnrow2, lntmp], lnrow2, tag=f"lnv2{k}")
        act(ap(lnrow2)[:, 0:T], ap(lnrow2)[:, 0:T], AF.Ln, [lnrow2, "eps1"],
            lnrow2, bias=ap("eps1"), scale=1.0, tag=f"lnl{k}")
        act(ap(lnrow2)[:, 0:T], ap(lnrow2)[:, 0:T], AF.Exp, [lnrow2, "mhalf"],
            lnrow2, scale=ap("mhalf"), tag=f"lne{k}")
        tt(ap(lnrow)[:, 0:T], ap(lnrow)[:, 0:T], ap(lnrow2)[:, 0:T], OP.mult,
           [lnrow, lnrow2], lnrow, tag=f"lnmr{k}")
        for s in range(nspl):
            c0, c1 = s * 512, min(T, (s + 1) * 512)
            mm(pap(3, c0, c1), ap("ones1"), ap(lnrow2)[:, c0:c1],
               ["ones1", lnrow2], "P3", True, True, tag=f"lnB1{k}")
            mm(pap(6, c0, c1), ap("ones1"), ap(lnrow)[:, c0:c1],
               ["ones1", lnrow], "P6", True, True, tag=f"lnB2{k}")
        for mc2 in range(DMC):
            tt(ap(lntmp)[:, 0:T], src_mt[mc2], pap(3, 0, T), OP.mult,
               [srcn(mc2), "P3"], lntmp, tag=f"lnA{k}")
            tt(dst32_mt[mc2], ap(lntmp)[:, 0:T], pap(6, 0, T), OP.subtract,
               [lntmp, "P6"], dstn(mc2), tag=f"lnO{k}",
               accum=(inplace or (not isinstance(d32name, list) and mc2 > 0)))
            if dst16_mt is not None:
                vcp(dst16_mt[mc2], dst32_mt[mc2], [dstn(mc2)], d16name,
                    tag=f"lnC{k}", accum=(mc2 > 0))

    def emit_ffn(src_name, dst_name, fcall):
        fstack = contextlib.ExitStack()
        sx = f"q{fcall}_"
        xin = MTT(sx + "xin", DMC, TF, stack=fstack)
        xf32 = MTT(sx + "xf32", DMC, TF, stack=fstack)
        xf16 = MTT(sx + "xf16", DMC, TF, BF16, stack=fstack)
        If = MTT(sx + "If", DMC, TF, stack=fstack)
        Vf = MTT(sx + "Vf", DMC, TF, stack=fstack)
        I16 = MTT(sx + "I16", DMC, TF, BF16, stack=fstack)
        V16 = MTT(sx + "V16", DMC, TF, BF16, stack=fstack)
        h116 = MTT(sx + "h116", 2, TF, BF16, stack=fstack)
        hh16 = MTT(sx + "hh16", 2, TF, BF16, stack=fstack)
        he16 = MTT(sx + "he16", NE, TF, BF16, prow=HH, stack=fstack)
        hr16 = MTT(sx + "hr16", 16, TF, BF16, stack=fstack)
        for nm, shape, dt in [
                ("logit", [NE, TF], F32), ("tm8", [128, NE], F32),
                ("topv8", [128, NE], F32), ("topi8", [128, NE], U32),
                ("topif", [128, 2], F32), ("wslot", [128, 2 * SUB], F32),
                ("mrow", [1, 2 * TF], F32), ("tb", [64, 2 * TF], BF16),
                ("m8", [8, 2 * TF], BF16), ("hem", [64, NE * 2 * TF], BF16),
                ("eb2Th", [8, 24 * 128], BF16), ("eb2Tl", [8, 24 * 128], BF16),
                ("wfm0", [1, TF], F32), ("wfm1", [1, TF], F32),
                ("hs16", [HH, TF], BF16),
                ("hlt", [1, TF], F32), ("x2st", [128, 2 * TF], F32),
                ("wb0s", [128, TF], F32), ("wb1s", [128, TF], F32),
                ("hbs", [128, TF], F32),
                ("lnrow", [1, 512], F32), ("lnrow2", [1, 512], F32),
                ("lntmp", [128, 512], F32),
                ("wt1", [128, 2 * DMC * 128], BF16),
                ("wt2", [128, 2 * 16 * 128], BF16)] + [
                (nm + sfx, shape, dt) for sfx in ("A", "B") for nm, shape, dt in [
                    ("sela", [128, 2 * TF], F32), ("selb", [128, 2 * TF], F32),
                    ("selg", [128, 2 * TF], F32),

                    ("vn0", [128, TF], F32), ("vn1", [128, TF], F32),
                    ("svt", [128, TF], F32), ("sxt", [128, TF], F32),
                    ("vsht", [128, TF], F32), ("xsht", [128, TF], F32),
                    ("xmu", [128, TF], F32),
                    ("ffA", [128, TF], F32), ("ffB", [128, TF], F32),
                    ("ew2b", [HH, NE * 3 * 128], BF16)]]:
            sbuf(sx + nm, shape, dt, stack=fstack)

        def a_(n):
            return ap(sx + n)

        def n_(n):
            return sx + n

        src_h = dram[src_name]
        dst_h = dram[dst_name]
        for part in range(3):
            for mc in range(DMC):
                pm = part * 8 + mc
                trans(pap(6, 0, 128, 0, 8),
                      bass.AP(sb["eb2s"], part * 8 + mc, [[NE * 24, 128], [24, NE]]),
                      ["eb2s"], "P6", np_=128, tag=f"Q{fcall}ebt")
                scop(a_("eb2Th")[:, pm * 128:(pm + 1) * 128],
                     pap(6, 0, 128, 0, 8), ["P6"], n_("eb2Th"),
                     tag=f"Q{fcall}ebh", accum=(pm > 0))
                tt(a_("eb2Tl")[:, pm * 128:(pm + 1) * 128],
                   pap(6, 0, 128, 0, 8),
                   a_("eb2Th")[:, pm * 128:(pm + 1) * 128], OP.subtract,
                   ["P6", n_("eb2Th")], n_("eb2Tl"), tag=f"Q{fcall}ebl",
                   accum=(pm > 0))
        for t_ in range(NFT):
            tc = slice(t_ * TF, (t_ + 1) * TF)
            tg = f"F{fcall}T{t_}"
            for mc in range(DMC):
                dma(xin[mc], src_h[mc * 128:(mc + 1) * 128, tc], n_("xin"),
                    rbufs=([src_name] if src_name != "xloc" else []),
                    tag=tg + f"lx{mc}", accum=(mc > 0))
            if FFN_UPTO < 1:
                continue
            emit_ln(xin, n_("xin"), xf32, n_("xf32"), xf16, n_("xf16"), TF, False,
                    n_("lnrow"), n_("lnrow2"), n_("lntmp"))
            if FFN_UPTO < 2:
                continue
            for mc2 in range(2):
                ph = 0 if mc2 == 0 else 7
                for kc in range(DMC):
                    mm(pap(ph, 0, TF, 0, 128),
                       rw1_16.cols(kc, mc2 * 128, (mc2 + 1) * 128), xf16[kc],
                       ["rw1_16", n_("xf16")], f"P{ph}", kc == 0, kc == DMC - 1,
                       tag=tg + "rh")
                act(h116[mc2], pap(ph, 0, TF, 0, 128), AF.Gelu, [f"P{ph}"],
                    n_("h116"), bias=ap("rb1s")[:, mc2:mc2 + 1], tag=tg + "rg",
                    accum=(mc2 > 0))
            for kc2 in range(2):
                mm(pap(1, 0, TF, 0, NE), ap("rw2_16")[:, kc2 * NE:(kc2 + 1) * NE],
                   h116[kc2], ["rw2_16", n_("h116")], "P1", kc2 == 0, kc2 == 1,
                   tag=tg + "rl")
            act(a_("logit")[:, 0:TF], pap(1, 0, TF, 0, NE), AF.Identity, ["P1"],
                n_("logit"), bias=ap("rb2s"), tag=tg + "rle")
            if FFN_UPTO < 3:
                continue
            for s in range(SUB):
                trans(pap(6, 0, NE, 0, 128), a_("logit")[:, s * 128:(s + 1) * 128],
                      [n_("logit")], "P6", np_=NE, tag=tg + "ttr")
                scop(a_("tm8"), pap(6, 0, NE, 0, 128), ["P6"], n_("tm8"),
                     tag=tg + "tme")
                vgen(lambda e: e.max(a_("topv8"), a_("tm8")),
                     [n_("tm8")], n_("topv8"), tag=tg + "mwi", accum=(s > 0))
                vgen(lambda e: e.max_index(a_("topi8"), a_("topv8"), a_("tm8")),
                     [n_("tm8"), n_("topv8")], n_("topi8"), tag=tg + "mwi2",
                     accum=(s > 0))
                tt(a_("ffAA")[:, 0:1], a_("topv8")[:, 0:1], a_("topv8")[:, 1:2],
                   OP.subtract, [n_("topv8")], n_("ffAA"), tag=tg + "d12")
                act(a_("wslot")[:, 2 * s:2 * s + 1], a_("ffAA")[:, 0:1], AF.Sigmoid,
                    [n_("ffAA")], n_("wslot"), tag=tg + "w0", accum=(s > 0))
                act(a_("wslot")[:, 2 * s + 1:2 * s + 2], a_("ffAA")[:, 0:1],
                    AF.Sigmoid, [n_("ffAA")], n_("wslot"), scale=-1.0, tag=tg + "w1",
                    accum=True)
                vcp(a_("topif"), a_("topi8")[:, 0:2], [n_("topi8")], n_("topif"),
                    tag=tg + "tif")
                for k_ in range(2):
                    trans(pap(6, 0, 128, 0, 1), a_("topif")[:, k_:k_ + 1],
                          [n_("topif")], "P6", np_=128, tag=tg + "mtr")
                    scop(a_("mrow")[0:1, k_ * TF + s * 128:k_ * TF + s * 128 + 128],
                         pap(6, 0, 128, 0, 1), ["P6"], n_("mrow"),
                         tag=tg + "mre", accum=(s + k_ > 0))
            for s in range(SUB):
                for k_ in range(2):
                    trans(pap(6, 0, 128, 0, 1),
                          a_("wslot")[:, 2 * s + k_:2 * s + k_ + 1],
                          [n_("wslot")], "P6", np_=128, tag=tg + "wtr")
                    scop(a_(f"wfm{k_}")[0:1, s * 128:(s + 1) * 128],
                         pap(6, 0, 128, 0, 1), ["P6"],
                         n_(f"wfm{k_}"), tag=tg + "wfm", accum=(s > 0))
            if FFN_UPTO < 4:
                continue
            for s2 in range((TF + 511) // 512):
                c0, c1 = s2 * 512, min(TF, (s2 + 1) * 512)
                mm(pap(4, c0, c1), ap("ones1"), a_("wfm0")[0:1, c0:c1],
                   ["ones1", n_("wfm0")], "P4", True, True, tag=tg + "wb0")
                mm(pap(5, c0, c1), ap("ones1"), a_("wfm1")[0:1, c0:c1],
                   ["ones1", n_("wfm1")], "P5", True, True, tag=tg + "wb1")
            scop(a_("wb0s"), pap(4, 0, TF), ["P4"], n_("wb0s"), tag=tg + "wbe0")
            scop(a_("wb1s"), pap(5, 0, TF), ["P5"], n_("wb1s"), tag=tg + "wbe1")
            for k_ in range(2):
                mm(pap(6, 0, TF, 0, 64), ap("ones1")[0:1, 0:64],
                   a_("mrow")[0:1, k_ * TF:(k_ + 1) * TF], ["ones1", n_("mrow")],
                   "P6", True, True, tag=tg + "tbb")
                scop(a_("tb")[:, k_ * TF:(k_ + 1) * TF], pap(6, 0, TF, 0, 64),
                     ["P6"], n_("tb"), tag=tg + "tbe", accum=(k_ > 0))
                tsc(a_("m8")[:, k_ * TF:(k_ + 1) * TF],
                    a_("tb")[0:8, k_ * TF:(k_ + 1) * TF],
                    ap("iotaS")[0:8, 0:1], OP.is_equal, [n_("tb"), "iotaS"],
                    n_("m8"), tag=tg + "m8", accum=(k_ > 0))
            memset(Vf.ap(), 0.0, n_("Vf"), tag=tg + "v0")
            memset(V16.ap(), 0.0, n_("V16"), tag=tg + "v160")
            if FFN_UPTO < 5:
                continue
            for it in range(1 if FFN_UPTO < 6 else 2):
                Imt, Iname = (xf32, n_("xf32")) if it == 0 else (If, n_("If"))
                Im16, I16n = (xf16, n_("xf16")) if it == 0 else (I16, n_("I16"))
                nkc = DMC if it == 0 else 2 * DMC

                def ctx16(kc):
                    if kc < DMC:
                        return Im16[kc], I16n
                    return V16[kc - DMC], n_("V16")

                if "hsh" in FFN_SK:
                    continue
                for kc in range(nkc):
                    rhs, rn = ctx16(kc)
                    mm(pap(0, 0, TF, 0, HH), shw1_16[kc], rhs, ["shw1_16", rn], "P0",
                       kc == 0, kc == nkc - 1, tag=tg + "sh1")
                act(a_("hs16")[:, 0:TF], pap(0, 0, TF, 0, HH), AF.Gelu, ["P0"],
                    n_("hs16"), bias=ap("shb1s"), tag=tg + "shg")
                if "hex" in FFN_SK:
                    continue
                for e_ in range(NE):
                    pe = 0 if e_ % 2 == 0 else 7
                    for kc in range(nkc):
                        rhs, rn = ctx16(kc)
                        mm(pap(pe, 0, TF, 0, HH), ew1_16[e_ * 16 + kc], rhs,
                           ["ew1_16", rn], f"P{pe}", kc == 0, kc == nkc - 1,
                           tag=tg + "eh")
                    act(he16[e_], pap(pe, 0, TF, 0, HH), AF.Gelu, [f"P{pe}"],
                        n_("he16"), bias=ap("eb1s")[:, e_:e_ + 1], tag=tg + "ehg",
                        accum=(e_ > 0))
                for k_ in range(2):
                    for e_ in range(NE):
                        stt(a_("hem")[:, (k_ * NE + e_) * TF:
                                      (k_ * NE + e_ + 1) * TF],
                            a_("tb")[:, k_ * TF:(k_ + 1) * TF], float(e_),
                            he16[e_], OP.is_equal, OP.mult,
                            [n_("tb"), n_("he16")], n_("hem"), tag=tg + "hm",
                            accum=(k_ + e_ > 0))
                if "halt" in FFN_SK:
                    continue
                for mc2 in range(2):
                    ph = 0 if mc2 == 0 else 7
                    for kc in range(DMC):
                        mm(pap(ph, 0, TF, 0, 128),
                           hw1_16.cols(kc, mc2 * 128, (mc2 + 1) * 128), Im16[kc],
                           ["hw1_16", I16n], f"P{ph}", kc == 0, kc == DMC - 1,
                           tag=tg + "hh")
                    act(hh16[mc2], pap(ph, 0, TF, 0, 128), AF.Gelu, [f"P{ph}"],
                        n_("hh16"), bias=ap("hb1s")[:, mc2:mc2 + 1], tag=tg + "hg",
                        accum=(mc2 > 0))
                for kc2 in range(2):
                    tsc(a_("ffAA")[:, 0:TF], hh16[kc2], ap("hw2f")[:, kc2:kc2 + 1],
                        OP.mult, [n_("hh16"), "hw2f"], n_("ffAA"), tag=tg + "hlx")
                    mm(pap(1, 0, TF, 0, 1), ap("onesc"), a_("ffAA")[:, 0:TF],
                       ["onesc", n_("ffAA")], "P1", kc2 == 0, kc2 == 1,
                       tag=tg + "hl")
                act(a_("hlt")[:, 0:TF], pap(1, 0, TF, 0, 1), AF.Sigmoid, ["P1"],
                    n_("hlt"), bias=ap("hb2s"), tag=tg + "hs")
                for s2 in range((TF + 511) // 512):
                    c0, c1 = s2 * 512, min(TF, (s2 + 1) * 512)
                    mm(pap(6, c0, c1), ap("ones1"), a_("hlt")[:, c0:c1],
                       ["ones1", n_("hlt")], "P6", True, True, tag=tg + "hb")
                scop(a_("hbs"), pap(6, 0, TF), ["P6"], n_("hbs"), tag=tg + "hbe")
                if "mc1" in FFN_SK:
                    continue
                for mc in range(DMC):
                    sfx = "A" if mc % 2 == 0 else "B"
                    ge = "vector" if mc % 2 == 0 else "gpsimd"
                    eb1, eb2_ = (1, 3) if mc % 2 == 0 else (0, 7)

                    def s_(nm):
                        return a_(nm + sfx)

                    def sn(nm):
                        return n_(nm + sfx)

                    dma(s_("ew2b"), dram["ew2_16"][mc * HH:(mc + 1) * HH, :],
                        sn("ew2b"), rbufs=["ew2_16"], tag=tg + "e2w",
                        accum=(it + mc // 2 > 0))
                    tsc(s_("xmu"), Imt[mc], ap("mus")[:, mc:mc + 1], OP.subtract,
                        [Iname, "mus"], sn("xmu"), tag=tg + "xmu")
                    for part in range(3):
                        if it == 0 and part == 0:
                            continue
                        pp = eb1 if part % 2 == 0 else eb2_
                        fc = part * 8 + mc
                        mm(pap(pp, 0, TF, 0, 128),
                           ap("shw2_16")[:, fc * 128:(fc + 1) * 128],
                           a_("hs16")[:, 0:TF], ["shw2_16", n_("hs16")], f"P{pp}",
                           True, True, tag=tg + "sh2")
                        bias_c = ap("shb2s")[:, fc:fc + 1]
                        if part == 0:
                            act(s_("vsht"), pap(pp, 0, TF, 0, 128), AF.Sigmoid,
                                [f"P{pp}"], sn("vsht"), bias=bias_c, tag=tg + "ssa")
                        elif part == 1:
                            softplus(s_("xsht"), pap(pp, 0, TF, 0, 128), bias_c,
                                     [f"P{pp}", "shb2s"], sn("xsht"),
                                     s_("ffA")[:, 0:TF], s_("ffB")[:, 0:TF],
                                     sn("ffA"), sn("ffB"), tag=tg + "ssb",
                                     engine=ge)
                        else:
                            act(s_("vn1"), pap(pp, 0, TF, 0, 128), AF.Sigmoid,
                                [f"P{pp}"], sn("vn1"), bias=bias_c, tag=tg + "ssg")
                    if "mc2" in FFN_SK:
                        continue
                    if it == 0:
                        stt(s_("svt"), s_("xsht"), -1.0, s_("xmu"), OP.mult, OP.mult,
                            [sn("xsht"), sn("xmu")], sn("svt"), tag=tg + "vs0")
                    else:
                        tt(s_("ffA")[:, 0:TF], s_("vsht"), Vf[mc], OP.mult,
                           [sn("vsht"), n_("Vf")], sn("ffA"), tag=tg + "vsa",
                           engine=ge)
                        tt(s_("ffB")[:, 0:TF], s_("xsht"), s_("xmu"), OP.mult,
                           [sn("xsht"), sn("xmu")], sn("ffB"), tag=tg + "vsb",
                           engine=ge)
                        tt(s_("svt"), s_("ffA")[:, 0:TF], s_("ffB")[:, 0:TF],
                           OP.subtract, [sn("ffA"), sn("ffB")], sn("svt"),
                           tag=tg + "vs1", engine=ge)
                    tt(s_("ffA")[:, 0:TF], s_("vn1"), s_("svt"), OP.mult,
                       [sn("vn1"), sn("svt")], sn("ffA"), tag=tg + "xs1", engine=ge)
                    stt(s_("xsht"), s_("ffA")[:, 0:TF], DT_CONST, Imt[mc], OP.mult,
                        OP.add, [sn("ffA"), Iname], sn("xsht"), tag=tg + "xs2")
                    vcp(s_("vsht"), s_("svt"), [sn("svt")], sn("vsht"),
                        tag=tg + "xs3", engine=ge)
                    if "mc3" in FFN_SK:
                        continue
                    for part, dst, dn in ((0, s_("sela"), sn("sela")),
                                          (1, s_("selb"), sn("selb")),
                                          (2, s_("selg"), sn("selg"))):
                        if it == 0 and part == 0:
                            continue
                        for k_ in range(2):
                            pp = eb1 if k_ == 0 else eb2_
                            for e_ in range(NE):
                                mm(pap(pp, 0, TF, 0, 128),
                                   s_("ew2b")[:, e_ * 384 + part * 128:
                                              e_ * 384 + (part + 1) * 128],
                                   a_("hem")[:, (k_ * NE + e_) * TF:
                                             (k_ * NE + e_ + 1) * TF],
                                   [sn("ew2b"), n_("hem")], f"P{pp}",
                                   e_ == 0, False, tag=tg + "e2")
                            pm = part * 8 + mc
                            for hl, ebt in ((0, "eb2Th"), (1, "eb2Tl")):
                                mm(pap(pp, 0, TF, 0, 128),
                                   a_(ebt)[:, pm * 128:(pm + 1) * 128],
                                   a_("m8")[:, k_ * TF:(k_ + 1) * TF],
                                   [n_(ebt), n_("m8")], f"P{pp}",
                                   False, hl == 1, tag=tg + "e2b")
                            sl = dst[:, k_ * TF:(k_ + 1) * TF]
                            if part == 0:
                                act(sl, pap(pp, 0, TF, 0, 128), AF.Sigmoid,
                                    [f"P{pp}"], dn, tag=tg + "sga",
                                    accum=(k_ > 0))
                            elif part == 2:
                                act(sl, pap(pp, 0, TF, 0, 128), AF.Sigmoid,
                                    [f"P{pp}"], dn, tag=tg + "sgg",
                                    accum=(k_ > 0))
                            else:
                                softplus(sl, pap(pp, 0, TF, 0, 128), 0.0,
                                         [f"P{pp}"], dn, None,
                                         s_("ffB")[:, 0:TF], None, sn("ffB"),
                                         tag=tg + "sgb", engine=ge,
                                         accum=(k_ > 0))
                    if "mc4" in FFN_SK:
                        continue

                    if "mc5" in FFN_SK:
                        continue
                    if it == 0:
                        stt(s_("vn0"), s_("selb")[:, 0:TF], -1.0, s_("xmu"), OP.mult,
                            OP.mult, [sn("selb"), sn("xmu")], sn("vn0"),
                            tag=tg + "cv0")
                        stt(s_("vn1"), s_("selb")[:, TF:2 * TF], -1.0, s_("xmu"),
                            OP.mult, OP.mult, [sn("selb"), sn("xmu")], sn("vn1"),
                            tag=tg + "cv1")
                    else:
                        tt(s_("vn0"), s_("sela")[:, 0:TF], Vf[mc], OP.mult,
                           [sn("sela"), n_("Vf")], sn("vn0"), tag=tg + "cva",
                           engine=ge)
                        tt(s_("ffA")[:, 0:TF], s_("selb")[:, 0:TF], s_("xmu"), OP.mult,
                           [sn("selb"), sn("xmu")], sn("ffA"), tag=tg + "cvb",
                           engine=ge)
                        tt(s_("vn0"), s_("vn0"), s_("ffA")[:, 0:TF], OP.subtract,
                           [sn("vn0"), sn("ffA")], sn("vn0"), tag=tg + "cvc",
                           engine=ge)
                        tt(s_("vn1"), s_("sela")[:, TF:2 * TF], Vf[mc], OP.mult,
                           [sn("sela"), n_("Vf")], sn("vn1"), tag=tg + "cvd",
                           engine=ge)
                        tt(s_("ffA")[:, 0:TF], s_("selb")[:, TF:2 * TF], s_("xmu"),
                           OP.mult, [sn("selb"), sn("xmu")], sn("ffA"),
                           tag=tg + "cve", engine=ge)
                        tt(s_("vn1"), s_("vn1"), s_("ffA")[:, 0:TF], OP.subtract,
                           [sn("vn1"), sn("ffA")], sn("vn1"), tag=tg + "cvf",
                           engine=ge)
                    tt(s_("svt"), s_("vn0"), a_("wb0s"), OP.mult,
                       [sn("vn0"), n_("wb0s")], sn("svt"), tag=tg + "sv0",
                       engine=ge)
                    tt(s_("ffA")[:, 0:TF], s_("vn1"), a_("wb1s"), OP.mult,
                       [sn("vn1"), n_("wb1s")], sn("ffA"), tag=tg + "sv1",
                       engine=ge)
                    tt(s_("sxt"), s_("selg")[:, 0:TF], s_("svt"), OP.mult,
                       [sn("selg"), sn("svt")], sn("sxt"), tag=tg + "sx0",
                       engine=ge)
                    tt(s_("ffB")[:, 0:TF], s_("selg")[:, TF:2 * TF],
                       s_("ffA")[:, 0:TF], OP.mult, [sn("selg"), sn("ffA")],
                       sn("ffB"), tag=tg + "sx1", engine=ge)
                    tt(s_("svt"), s_("svt"), s_("ffA")[:, 0:TF], OP.add,
                       [sn("svt"), sn("ffA")], sn("svt"), tag=tg + "sv2",
                       engine=ge)
                    tt(s_("sxt"), s_("sxt"), s_("ffB")[:, 0:TF], OP.add,
                       [sn("sxt"), sn("ffB")], sn("sxt"), tag=tg + "sx4",
                       engine=ge)

                    stt(s_("ffA")[:, 0:TF], s_("sxt"), DT_CONST, Imt[mc], OP.mult,
                        OP.add, [sn("sxt"), Iname], sn("ffA"), tag=tg + "ix")
                    tsc(s_("ffB")[:, 0:TF], s_("xsht"), ap("swcol"), OP.mult,
                        [sn("xsht"), "swcol"], sn("ffB"), tag=tg + "ib")
                    stt(s_("ffA")[:, 0:TF], s_("ffA")[:, 0:TF], ap("onemsw"),
                        s_("ffB")[:, 0:TF], OP.mult, OP.add,
                        [sn("ffA"), "onemsw", sn("ffB")], sn("ffA"), tag=tg + "ic")
                    tt(s_("ffA")[:, 0:TF], s_("ffA")[:, 0:TF], Imt[mc], OP.subtract,
                       [sn("ffA"), Iname], sn("ffA"), tag=tg + "id", engine=ge)
                    tt(s_("ffA")[:, 0:TF], s_("ffA")[:, 0:TF], a_("hbs"),
                       OP.mult, [sn("ffA"), n_("hbs")], sn("ffA"), tag=tg + "ie",
                       engine=ge)
                    tt(If[mc], Imt[mc], s_("ffA")[:, 0:TF], OP.add,
                       [Iname, sn("ffA")], n_("If"), tag=tg + "if",
                       accum=(mc > 0 or it > 0), engine=ge)
                    tsc(s_("ffA")[:, 0:TF], s_("vsht"), ap("swcol"), OP.mult,
                        [sn("vsht"), "swcol"], sn("ffA"), tag=tg + "iv1")
                    stt(s_("ffA")[:, 0:TF], s_("svt"), ap("onemsw"),
                        s_("ffA")[:, 0:TF], OP.mult, OP.add,
                        [sn("svt"), "onemsw", sn("ffA")], sn("ffA"), tag=tg + "iv2")
                    if it == 0:
                        tt(Vf[mc], s_("ffA")[:, 0:TF], a_("hbs"), OP.mult,
                           [sn("ffA"), n_("hbs")], n_("Vf"), tag=tg + "iv3",
                           accum=True, engine=ge)
                    else:
                        tt(s_("ffA")[:, 0:TF], s_("ffA")[:, 0:TF], Vf[mc],
                           OP.subtract, [sn("ffA"), n_("Vf")], sn("ffA"),
                           tag=tg + "iv4", engine=ge)
                        tt(s_("ffA")[:, 0:TF], s_("ffA")[:, 0:TF], a_("hbs"),
                           OP.mult, [sn("ffA"), n_("hbs")], sn("ffA"),
                           tag=tg + "iv5", engine=ge)
                        tt(Vf[mc], Vf[mc], s_("ffA")[:, 0:TF], OP.add,
                           [n_("Vf"), sn("ffA")], n_("Vf"), tag=tg + "iv6",
                           accum=True, engine=ge)
                if it == 0:
                    for mc in range(DMC):
                        vcp(I16[mc], If[mc], [n_("If")], n_("I16"), tag=tg + "ci",
                            accum=(mc > 0))
                        vcp(V16[mc], Vf[mc], [n_("Vf")], n_("V16"), tag=tg + "cw",
                            accum=(mc > 0))
                if it == 1 and dbg and fcall == 0:
                    for mc in range(DMC):
                        dma(dram["dbgI"][mc * 128:(mc + 1) * 128, tc], If[mc],
                            "dbgI", chan="dbgI", rbufs=[n_("If")],
                            tag=tg + "dI", accum=True)
                        dma(dram["dbgV"][mc * 128:(mc + 1) * 128, tc], Vf[mc],
                            "dbgV", chan="dbgV", rbufs=[n_("Vf")],
                            tag=tg + "dV", accum=True)
                    if False:
                        pass

            if FFN_UPTO < 7:
                continue
            for mc in range(DMC):
                vcp(I16[mc], If[mc], [n_("If")], n_("I16"), tag=tg + "cf",
                    accum=(mc > 0))
            for mc2 in range(16):
                sl_ = mc2 % 2
                wof = sl_ * DMC * 128
                pr = 0 if mc2 % 2 == 0 else 7
                dma(a_("wt1")[:, wof:wof + DMC * 128],
                    bass.AP(dram["rfw1_16"].tensor, mc2 * 128,
                            [[2 * DM, 128], [128 * 2 * DM, DMC], [1, 128]]),
                    n_("wt1") + str(sl_), rbufs=["rfw1_16"], tag=tg + "rf1w")
                for kc in range(DMC):
                    mm(pap(pr, 0, TF, 0, 128),
                       a_("wt1")[:, wof + kc * 128:wof + (kc + 1) * 128], I16[kc],
                       [n_("wt1") + str(sl_), n_("I16")],
                       f"P{pr}", kc == 0, kc == DMC - 1, tag=tg + "rf1")
                act(hr16[mc2], pap(pr, 0, TF, 0, 128), AF.Gelu, [f"P{pr}"],
                    n_("hr16"), bias=ap("rfb1s")[:, mc2:mc2 + 1], tag=tg + "rfg",
                    accum=(mc2 > 0))
            for mc in range(DMC):
                sl_ = mc % 2
                wof = sl_ * 16 * 128
                pr = 1 if mc % 2 == 0 else 3
                dma(a_("wt2")[:, wof:wof + 16 * 128],
                    bass.AP(dram["rfw2_16"].tensor, mc * 128,
                            [[DM, 128], [128 * DM, 16], [1, 128]]),
                    n_("wt2") + str(sl_), rbufs=["rfw2_16"], tag=tg + "rf2w")
                for kc in range(16):
                    mm(pap(pr, 0, TF, 0, 128),
                       a_("wt2")[:, wof + kc * 128:wof + (kc + 1) * 128], hr16[kc],
                       [n_("wt2") + str(sl_), n_("hr16")],
                       f"P{pr}", kc == 0, kc == 15, tag=tg + "rf2")
                rsfx = "A" if mc % 2 == 0 else "B"
                rfA, rfAn = a_("ffA" + rsfx), n_("ffA" + rsfx)
                act(rfA[:, 0:TF], pap(pr, 0, TF, 0, 128), AF.Identity,
                    [f"P{pr}"], rfAn, bias=ap("rfb2s")[:, mc:mc + 1],
                    tag=tg + "rfe")
                stt(rfA[:, 0:TF], rfA[:, 0:TF], ap("iws")[:, mc:mc + 1],
                    xf32[mc], OP.mult, OP.add, [rfAn, "iws", n_("xf32")],
                    rfAn, tag=tg + "rfi")
                stn = n_("x2st") + str(mc % 2)
                st_ap = a_("x2st")[:, (mc % 2) * TF:(mc % 2 + 1) * TF]
                stt(st_ap, rfA[:, 0:TF], 0.5, xin[mc], OP.mult, OP.add,
                    [rfAn, n_("xin")], stn, tag=tg + "rfo")
                dma(dst_h[mc * 128:(mc + 1) * 128, tc], st_ap, dst_name,
                    rbufs=[stn], tag=tg + "st", accum=True, chan=stn)
        fstack.close()

    def emit_mamba(p):
        w = f"p{p}"
        mstack = contextlib.ExitStack()
        sx = f"m{p}_"
        hmb = MTT(sx + "hmb", DMC, TM, BF16, stack=mstack)
        xzx = MTT(sx + "xzx", DIC, TM + KC - 1, stack=mstack)
        xcv = MTT(sx + "xcv", DIC, TM, stack=mstack)
        xcv16 = MTT(sx + "xcv16", DIC, TM, BF16, stack=mstack)
        dtb = MTT(sx + "dtb", DIC, TM, stack=mstack)
        szc = MTT(sx + "szc", DIC, TM, stack=mstack)
        gated = MTT(sx + "gated", DIC, TM, BF16, stack=mstack)
        Ball = MTT(sx + "Ball", NST, TM, BF16, stack=mstack)
        Call = MTT(sx + "Call", NST, TM, BF16, stack=mstack)
        oev = MTT(sx + "oev", DMC, TM, stack=mstack)
        x2c = oev
        x2r = MTT(sx + "x2r", 2, TM, stack=mstack) if p == 2 else None
        o1r = MTT(sx + "o1r", 2, TM, stack=mstack) if p == 2 else None
        for nm, shape, dt in [
                ("dtxs", [128, TM], F32), ("xdbl", [RK + 2 * NST, TM], F32),
                ("xdbl16", [RK, TM], BF16),
                ("dA2", [128, 2 * TM], F32), ("dB2", [128, 2 * TM], F32),
                ("hc2", [128, 2 * TM], F32), ("gq2", [128, 2 * TM], BF16),
                ("bcr", [1, 8 * TM], F32),
                ("mtA", [128, TM], F32), ("mtB", [128, TM], F32),
                ("mtC", [128, TM], F32), ("mtD", [128, TM], F32),
                ("wt1", [128, 2 * DMC * 128], BF16),
                ("wt2", [128, 2 * DIC * 128], BF16)]:
            sbuf(sx + nm, shape, dt, stack=mstack)
        if p == 1:
            for nm, shape, dt in [("hbst", [128, DMC * TM], BF16),
                                  ("lnrow", [1, 512], F32),
                                  ("lnrow2", [1, 512], F32),
                                  ("lntmp", [128, 512], F32)]:
                sbuf(sx + nm, shape, dt, stack=mstack)

        def a_(n):
            return ap(sx + n)

        def n_(n):
            return sx + n

        if p == 1:
            memset(ap("state"), 0.0, "state", tag="st0")

        def chcols(ch):
            if p == 1:
                tcols = slice(ch * TM, (ch + 1) * TM)
            else:
                tcols = slice(OWN - (ch + 1) * TM, OWN - ch * TM)
            scols = slice(ch * TM, (ch + 1) * TM)
            mcol = slice(OWN - (ch + 1) * TM, OWN - ch * TM)
            return f"M{p}C{ch}", tcols, scols, mcol

        def s0(ch):
            tg, tcols, scols, mcol = chcols(ch)
            if p == 1:
                for mc in range(DMC):
                    dma(x2c[mc], dram["x2_hbm"][mc * 128:(mc + 1) * 128, tcols],
                        n_("oev") + str(mc), rbufs=["x2_hbm"], tag=tg + "lx2")
                emit_ln(x2c, [n_("oev") + str(q) for q in range(DMC)],
                        x2c, [n_("oev") + str(q) for q in range(DMC)],
                        hmb, n_("hmb"), TM, True,
                        n_("lnrow"), n_("lnrow2"), n_("lntmp"))
                for mc in range(DMC):
                    hb = a_("hbst")[:, mc * TM:(mc + 1) * TM]
                    vcp(hb, hmb[mc][:, ::-1], [n_("hmb")], n_("hbst") + str(mc),
                        tag=tg + "hrv")
                    dma(dram["h_hbm"][mc * 128:(mc + 1) * 128, mcol], hb, "h_hbm",
                        rbufs=[n_("hbst") + str(mc)], tag=tg + "sh", accum=True,
                        chan=n_("hbst") + str(mc))
            else:
                for mc in range(DMC):
                    dma(hmb[mc], dram["h_hbm"][mc * 128:(mc + 1) * 128, scols],
                        n_("hmb"), rbufs=["h_hbm"], tag=tg + "lh", accum=(mc > 0))

        def s1x(ch, elist):
            tg, _, _, _ = chcols(ch)
            for ec in elist:
                pb = 4 + (ec % 2)
                sl_ = ec % 2
                wof = sl_ * DMC * 128
                dma(a_("wt1")[:, wof:wof + DMC * 128],
                    bass.AP(dram["WinT16"].tensor, ec * 128,
                            [[2 * DI, 128], [128 * 2 * DI, DMC], [1, 128]]),
                    n_("wt1") + str(sl_), rbufs=["WinT16"], tag=tg + "ipw")
                for kc in range(DMC):
                    mm(pap(pb, 0, TM, 0, 128),
                       a_("wt1")[:, wof + kc * 128:wof + (kc + 1) * 128], hmb[kc],
                       [n_("wt1") + str(sl_), n_("hmb")],
                       f"P{pb}", kc == 0, kc == DMC - 1, tag=tg + "ip")
                scop(xzx.cols(ec, KC - 1, KC - 1 + TM), pap(pb, 0, TM),
                     [f"P{pb}"], n_("xzx"), tag=tg + "ipe", accum=(ec > 0))

        def s1z(ch):
            tg, tcols, scols, mcol = chcols(ch)
            if p == 1:
                for ec in range(DIC, 2 * DIC):
                    pb = 4 + (ec % 2)
                    sl_ = ec % 2
                    wof = sl_ * DMC * 128
                    dma(a_("wt1")[:, wof:wof + DMC * 128],
                        bass.AP(dram["WinT16"].tensor, ec * 128,
                                [[2 * DI, 128], [128 * 2 * DI, DMC], [1, 128]]),
                        n_("wt1") + str(sl_), rbufs=["WinT16"], tag=tg + "ipw")
                    for kc in range(DMC):
                        mm(pap(pb, 0, TM, 0, 128),
                           a_("wt1")[:, wof + kc * 128:wof + (kc + 1) * 128],
                           hmb[kc], [n_("wt1") + str(sl_), n_("hmb")],
                           f"P{pb}", kc == 0, kc == DMC - 1, tag=tg + "ip")
                    act(szc[ec - DIC], pap(pb, 0, TM), AF.Silu, [f"P{pb}"],
                        n_("szc"), tag=tg + "ips", accum=(ec > DIC))
                for dc in range(DIC):
                    sz_st = dtb[dc]
                    szn = n_("dtb") + str(dc)
                    vcp(sz_st, szc[dc][:, ::-1], [n_("szc")], szn, tag=tg + "szr")
                    dma(dram["sz_hbm"][dc * 128:(dc + 1) * 128, mcol], sz_st,
                        "sz_hbm", rbufs=[szn], tag=tg + "ssz", accum=True,
                        chan=szn)
            else:
                for dc in range(DIC):
                    dma(szc[dc], dram["sz_hbm"][dc * 128:(dc + 1) * 128, scols],
                        n_("szc"), rbufs=["sz_hbm"], tag=tg + "lsz",
                        accum=(dc > 0))

        def s2(ch):
            tg, _, _, _ = chcols(ch)
            if ch == 0 and p == 1:
                memset(ap("carry"), 0.0, "carry", tag=tg + "c0")
            for dc in range(DIC):
                vcp(xzx.cols(dc, 0, KC - 1), ap("carry")[:, dc * 3:dc * 3 + 3],
                    ["carry"], n_("xzx"), tag=tg + "cp", accum=True)
            for dc in range(DIC):
                vcp(ap("carry")[:, dc * 3:dc * 3 + 3], xzx.cols(dc, TM, TM + KC - 1),
                    [n_("xzx")], "carry", tag=tg + "cu", accum=(dc > 0))
            for dc in range(DIC):
                cac = "mtA" if dc % 2 == 0 else "mtC"
                tsc(a_(cac), xzx.cols(dc, 0, TM),
                    ap(f"s_convw_{w}")[:, dc * KC:dc * KC + 1], OP.mult,
                    [n_("xzx"), f"s_convw_{w}"], n_(cac), tag=tg + "cv0")
                for k_ in range(1, KC):
                    stt(a_(cac), xzx.cols(dc, k_, k_ + TM),
                        ap(f"s_convw_{w}")[:, dc * KC + k_:dc * KC + k_ + 1],
                        a_(cac), OP.mult, OP.add,
                        [n_("xzx"), f"s_convw_{w}", n_(cac)], n_(cac),
                        tag=tg + f"cv{k_}")
                act(xcv[dc], a_(cac), AF.Silu, [n_(cac), f"s_convb_{w}"],
                    n_("xcv"), bias=ap(f"s_convb_{w}")[:, dc:dc + 1],
                    tag=tg + "cvs", accum=(dc > 0))
                act(xcv16[dc], a_(cac), AF.Silu, [n_(cac), f"s_convb_{w}"],
                    n_("xcv16"), bias=ap(f"s_convb_{w}")[:, dc:dc + 1],
                    tag=tg + "cvc", accum=(dc > 0))
            for dc in range(DIC):
                mm(pap(7, 0, TM, 0, RK + 2 * NST), WxT16[w][dc], xcv16[dc],
                   [f"WxT16_{w}", n_("xcv16")], "P7", dc == 0, dc == DIC - 1,
                   tag=tg + "xd")
            scop(a_("xdbl")[:, 0:TM], pap(7, 0, TM, 0, RK + 2 * NST), ["P7"],
                 n_("xdbl"), tag=tg + "xde")
            vcp(a_("xdbl16")[:, 0:TM], a_("xdbl")[0:RK, 0:TM], [n_("xdbl")],
                n_("xdbl16"), tag=tg + "xdc")
            for grp in range(4):
                ns0 = grp * 4
                dma(a_("bcr")[0:1, 0:4 * TM],
                    bass.AP(sb[n_("xdbl")], (RK + ns0) * TM, [[TM, 4], [1, TM]]),
                    n_("bcr"), rbufs=[n_("xdbl")], tag=tg + "bcB",
                    accum=(grp > 0))
                dma(a_("bcr")[0:1, 4 * TM:8 * TM],
                    bass.AP(sb[n_("xdbl")], (RK + NST + ns0) * TM,
                            [[TM, 4], [1, TM]]),
                    n_("bcr"), rbufs=[n_("xdbl")], tag=tg + "bcC",
                    accum=True)
                for half in range(2):
                    pb_ = (4, 0)[half]
                    pc_ = (5, 1)[half]
                    c0 = half * 2 * TM
                    mm(pap(pb_, 0, 2 * TM), ap("ones1"),
                       a_("bcr")[0:1, c0:c0 + 2 * TM], ["ones1", n_("bcr")],
                       f"P{pb_}", True, True, tag=tg + "bb")
                    mm(pap(pc_, 0, 2 * TM), ap("ones1"),
                       a_("bcr")[0:1, c0 + 4 * TM:c0 + 6 * TM],
                       ["ones1", n_("bcr")], f"P{pc_}", True, True,
                       tag=tg + "cb")
                    nsq = ns0 + 2 * half
                    scop(Ball.cols(nsq, 0, 2 * TM), pap(pb_, 0, 2 * TM),
                         [f"P{pb_}"], n_("Ball"), tag=tg + "be",
                         accum=(grp + half > 0))
                    scop(Call.cols(nsq, 0, 2 * TM), pap(pc_, 0, 2 * TM),
                         [f"P{pc_}"], n_("Call"), tag=tg + "ce",
                         accum=(grp + half > 0))
            for dcg in range(8):
                g0 = dcg * 2
                dnames = [n_("dtb") + str(g0 + q) for q in range(2)]
                xnames = [n_("dA2") + "0", n_("dA2") + "1"]
                for q in range(2):
                    dc = g0 + q
                    pd = 3 if dc % 2 == 0 else 6
                    mm(pap(pd, 0, TM, 0, 128),
                       ap(f"WdtT16_{w}")[:, dc * 128:(dc + 1) * 128],
                       a_("xdbl16")[:, 0:TM], [f"WdtT16_{w}", n_("xdbl16")],
                       f"P{pd}", True, True, tag=tg + "dt")
                    act(dtb[dc], pap(pd, 0, TM, 0, 128), AF.Identity,
                        [f"P{pd}", f"s_bdt_{w}"], dnames[q],
                        bias=ap(f"s_bdt_{w}")[:, dc:dc + 1], tag=tg + "dte")
                gsl = dtb.t.ap()[:, g0 * TM:(g0 + 2) * TM]
                ir.op("vector",
                      lambda e, o=a_("dA2"), i=gsl: e.tensor_scalar(
                          out=o, in0=i, scalar1=0.0, scalar2=None, op0=OP.min),
                      reads=[B(nm) for nm in dnames],
                      writes=[B(nm) for nm in xnames], tag=tg + "spm")
                ir.op("scalar",
                      lambda e, o=a_("dA2"): e.activation(out=o, in_=o,
                                                          func=AF.Exp),
                      reads=[B(nm) for nm in xnames],
                      writes=[B(nm) for nm in xnames], tag=tg + "spe")
                ir.op("scalar",
                      lambda e, o=a_("dA2"): e.activation(out=o, in_=o,
                                                          func=AF.Ln,
                                                          bias=1.0, scale=1.0),
                      reads=[B(nm) for nm in xnames],
                      writes=[B(nm) for nm in xnames], tag=tg + "spl")
                ir.op("vector",
                      lambda e, o=gsl: e.tensor_scalar(
                          out=o, in0=o, scalar1=0.0, scalar2=None, op0=OP.max),
                      reads=[B(nm) for nm in dnames],
                      writes=[B(nm) for nm in dnames], tag=tg + "spx")
                ir.op("vector",
                      lambda e, o=gsl, x=a_("dA2"): e.tensor_tensor(
                          out=o, in0=o, in1=x, op=OP.add),
                      reads=[B(nm) for nm in dnames + xnames],
                      writes=[B(nm) for nm in dnames], tag=tg + "spa")

        def s3(ch, inject):
            tg, _, _, _ = chcols(ch)
            for dc in range(DIC):
                tt(a_("dtxs"), dtb[dc], xcv[dc], OP.mult,
                   [n_("dtb") + str(dc), n_("xcv")], n_("dtxs"), tag=tg + "dtx",
                   engine="gpsimd")
                yb = 2 if dc % 2 == 0 else 7
                for ns in range(NST):
                    h_ = (ns % 2) * TM
                    hb = str(ns % 2)
                    act(a_("dA2")[:, h_:h_ + TM], dtb[dc], AF.Exp,
                        [n_("dtb") + str(dc), f"s_A_{w}"], n_("dA2") + hb,
                        scale=ap(f"s_A_{w}")[:, dc * NST + ns:dc * NST + ns + 1],
                        tag=tg + "dA")
                    tt(a_("dB2")[:, h_:h_ + TM], a_("dtxs"), Ball[ns],
                       OP.mult, [n_("dtxs"), n_("Ball")], n_("dB2") + hb,
                       tag=tg + "dB", engine="gpsimd")
                    dcn = dc * NST + ns
                    vgen(lambda e, hh_=h_, q=dcn: e.tensor_tensor_scan(
                             out=a_("hc2")[:, hh_:hh_ + TM],
                             data0=a_("dA2")[:, hh_:hh_ + TM],
                             data1=a_("dB2")[:, hh_:hh_ + TM],
                             initial=ap("state")[:, q:q + 1],
                             op0=OP.mult, op1=OP.add),
                         [n_("dA2") + hb, n_("dB2") + hb, "state"],
                         n_("hc2") + hb, tag=tg + "sc")
                    if ns % 2 == 1:
                        vcp(ap("state")[:, dcn - 1:dcn + 1],
                            bass.AP(sb[n_("hc2")], TM - 1,
                                    [[2 * TM, 128], [TM, 2]]),
                            [n_("hc2") + "0", n_("hc2") + "1"], "state",
                            tag=tg + "su", accum=True)
                    tt(a_("gq2")[:, h_:h_ + TM], a_("hc2")[:, h_:h_ + TM],
                       Call[ns], OP.mult, [n_("hc2") + hb, n_("Call")],
                       n_("gq2") + hb, tag=tg + "g",
                       engine=("gpsimd" if ns % 2 == 1 else "vector"))
                    mm(pap(yb, 0, TM), ap("ident16"), a_("gq2")[:, h_:h_ + TM],
                       ["ident16", n_("gq2") + hb], f"P{yb}", ns == 0,
                       ns == NST - 1, tag=tg + "ga")
                ydt = "mtB" if dc % 2 == 0 else "mtD"
                stt(a_(ydt), xcv[dc], ap(f"s_Dp_{w}")[:, dc:dc + 1],
                    pap(yb, 0, TM), OP.mult, OP.add,
                    [n_("xcv"), f"s_Dp_{w}", f"P{yb}"], n_(ydt), tag=tg + "yd")
                tt(gated[dc], a_(ydt), szc[dc], OP.mult, [n_(ydt), n_("szc")],
                   n_("gated"), tag=tg + "gt", accum=(dc > 0))
                inject(dc)

        def s4(ch):
            tg, tcols, scols, mcol = chcols(ch)
            for mc in range(DMC):
                sl_ = mc % 2
                wof = sl_ * DIC * 128
                po = 6 if mc % 2 == 0 else 5
                dma(a_("wt2")[:, wof:wof + DIC * 128],
                    bass.AP(dram["WoutT16"].tensor, mc * 128,
                            [[DM, 128], [128 * DM, DIC], [1, 128]]),
                    n_("wt2") + str(sl_), rbufs=["WoutT16"], tag=tg + "opw")
                for dc in range(DIC):
                    mm(pap(po, 0, TM),
                       a_("wt2")[:, wof + dc * 128:wof + (dc + 1) * 128],
                       gated[dc], [n_("wt2") + str(sl_), n_("gated")],
                       f"P{po}", dc == 0, dc == DIC - 1, tag=tg + "op")
                if p == 1:
                    scop(oev[mc], pap(po, 0, TM), [f"P{po}"],
                         n_("oev") + str(mc), tag=tg + "ope")
                    dma(dram["o1_hbm"][mc * 128:(mc + 1) * 128, tcols], oev[mc],
                        "o1_hbm", rbufs=[n_("oev") + str(mc)], tag=tg + "so1",
                        accum=True, chan=n_("oev") + str(mc))
                else:
                    dma(x2r[sl_], dram["x2_hbm"][mc * 128:(mc + 1) * 128, tcols],
                        n_("x2r") + str(sl_), rbufs=["x2_hbm"], tag=tg + "lx2r")
                    dma(o1r[sl_], dram["o1_hbm"][mc * 128:(mc + 1) * 128, tcols],
                        n_("o1r") + str(sl_), rbufs=["o1_hbm"],
                        tag=tg + "lo1r")
                    tt(oev[mc], x2r[sl_][:, ::-1], o1r[sl_][:, ::-1], OP.add,
                       [n_("x2r") + str(sl_), n_("o1r") + str(sl_)],
                       n_("oev") + str(mc), tag=tg + "x3a")
                    tt(oev[mc], oev[mc], pap(po, 0, TM), OP.add,
                       [n_("oev") + str(mc), f"P{po}"], n_("oev") + str(mc),
                       tag=tg + "x3b", accum=True)
                    dma(dram["x3_hbm"][mc * 128:(mc + 1) * 128, scols], oev[mc],
                        "x3_hbm", rbufs=[n_("oev") + str(mc)], tag=tg + "sx3",
                        accum=True, chan=n_("oev") + str(mc))

        def inj_mk(nxt):
            def inj(dc):
                if nxt >= NMC:
                    return
                if dc == 0:
                    s0(nxt)
                elif 1 <= dc <= 8:
                    s1x(nxt, [2 * (dc - 1), 2 * (dc - 1) + 1])
            return inj

        s0(0)
        s1x(0, list(range(DIC)))
        s1z(0)
        s2(0)
        for ch in range(NMC):
            s3(ch, inj_mk(ch + 1))
            s4(ch)
            if ch + 1 < NMC:
                s1z(ch + 1)
                s2(ch + 1)
        if p == 1:
            vcp(ap("agin")[:, 0:DIC * NST], ap("state"), ["state"], "ago0",
                tag="agst")
            for dc in range(DIC):
                vcp(ap("agin")[:, DIC * NST + dc * 3:DIC * NST + dc * 3 + 3],
                    xzx.cols(dc, TM, TM + KC - 1), [n_("xzx")], "ago0", tag="aghl",
                    accum=True)
            dma(dram["cin"], ap("agin"), "cin", rbufs=["ago0"], tag="agci")
            ir.op("gpsimd",
                  lambda e: e.collective_compute(
                      "AllGather", OP.bypass,
                      replica_groups=[[0, 1], [2, 3], [4, 5], [6, 7]],
                      ins=[dram["cin"].opt()], outs=[dram["cout"].opt()]),
                  reads=[B("cin")], writes=[B("cout")], tag="ag")
            dma(ap("ago0"), dram["cout"][0:128, :], "ago0", rbufs=["cout"], tag="ag0")
            dma(ap("ago1"), dram["cout"][128:256, :], "ago1", rbufs=["cout"],
                tag="ag1")
            tsc(ap("ago1"), ap("ago1"), ap("selm")[:, 1:2], OP.mult,
                ["ago1", "selm"], "ago1", tag="agm1")
            stt(ap("ago1"), ap("ago0"), ap("selm")[:, 0:1], ap("ago1"), OP.mult,
                OP.add, ["ago0", "selm", "ago1"], "ago1", tag="agm0")
            vcp(ap("state"), ap("ago1")[:, 0:DIC * NST], ["ago1"], "state",
                tag="ags2")
            for dc in range(DIC):
                src = ap("ago1")[:, DIC * NST + dc * 3:DIC * NST + dc * 3 + 3]
                vcp(ap("carry")[:, dc * 3:dc * 3 + 3], src, ["ago1"],
                    "carry", tag="agc2", accum=(dc > 0))
        mstack.close()

    if phases & 1:
        emit_ffn("xloc", "x2_hbm", 0)
        ir.barrier("b0")
    if phases & 2:
        emit_mamba(1)
        ir.barrier("b1")
    if phases & 4:
        emit_mamba(2)
        ir.barrier("b2")
    if phases & 8:
        emit_ffn("x3_hbm", "out", 1)
    else:
        srcn = "x3_hbm" if (phases & 4) else "x2_hbm"
        sbuf("copybb", [128, OWN], F32)
        for mc in range(DMC):
            dma(sb["copybb"].ap(), dram[srcn][mc * 128:(mc + 1) * 128, :], "copybb",
                rbufs=[srcn], tag=f"cb{mc}")
            dma(dram["out"][mc * 128:(mc + 1) * 128, :], sb["copybb"].ap(), "out",
                rbufs=["copybb"], tag=f"cbo{mc}", accum=(mc > 0))
    ir.op("gpsimd", lambda e: e.nop(), reads=[B("out")], tag="final")

    _build_block(nc, ir)
    gstack.close()
    return nc, ir



def _prep_core_inputs(inp, OWN):
    f32 = np.float32
    x = np.asarray(inp["L0_attn_hidden_states"], f32)
    W_in = np.asarray(inp["L0_attn_W_in"], f32)
    W_out = np.asarray(inp["L0_attn_W_out"], f32)

    def pack_col(a, nch):
        return np.ascontiguousarray(np.asarray(a, f32).reshape(nch, 128).T)

    def pack_cols(a, nch):
        a = np.asarray(a, f32)
        w = a.shape[1]
        return np.ascontiguousarray(a.reshape(nch, 128, w).transpose(1, 0, 2)
                                    .reshape(128, nch * w))

    ew2 = np.asarray(inp["L0_ffn_expert_w2"], f32)
    ew2mc = np.empty((DMC * HH, NE * 3 * 128), f32)
    for mc in range(DMC):
        for e_ in range(NE):
            for part in range(3):
                ew2mc[mc * HH:(mc + 1) * HH,
                      e_ * 384 + part * 128:e_ * 384 + (part + 1) * 128] = \
                    ew2[e_, :, part * DM + mc * 128: part * DM + (mc + 1) * 128]

    shared = {
        "rw1": np.ascontiguousarray(inp["L0_ffn_router_w1"], f32),
        "rb1": pack_col(inp["L0_ffn_router_b1"], 2),
        "rw2": np.ascontiguousarray(inp["L0_ffn_router_w2"], f32),
        "rb2": np.asarray(inp["L0_ffn_router_b2"], f32).reshape(NE, 1),
        "hw1": np.ascontiguousarray(inp["L0_ffn_halt_w1"], f32),
        "hb1": pack_col(inp["L0_ffn_halt_b1"], 2),
        "hw2": np.ascontiguousarray(inp["L0_ffn_halt_w2"], f32),
        "hb2": np.asarray(inp["L0_ffn_halt_b2"], f32).reshape(1, 1),
        "shw1": np.ascontiguousarray(inp["L0_ffn_shared_w1"], f32),
        "shb1": np.asarray(inp["L0_ffn_shared_b1"], f32).reshape(HH, 1),
        "shw2": np.ascontiguousarray(inp["L0_ffn_shared_w2"], f32),
        "shb2": pack_col(inp["L0_ffn_shared_b2"], 24),
        "ew1": np.ascontiguousarray(inp["L0_ffn_expert_w1"], f32),
        "eb1": np.ascontiguousarray(np.asarray(inp["L0_ffn_expert_b1"], f32).T),
        "ew2mc": ew2mc,
        "eb2": np.ascontiguousarray(
            np.asarray(inp["L0_ffn_expert_b2"], f32).reshape(NE, 24, 128)
            .transpose(2, 0, 1).reshape(128, NE * 24)),
        "rfw1": np.ascontiguousarray(inp["L0_ffn_refine_w1"], f32),
        "rfb1": pack_col(inp["L0_ffn_refine_b1"], 16),
        "rfw2": np.ascontiguousarray(inp["L0_ffn_refine_w2"], f32),
        "rfb2": pack_col(inp["L0_ffn_refine_b2"], 8),
        "swt": np.asarray(inp["L0_ffn_shared_weight"], f32).reshape(1, 1),
        "mu": pack_col(inp["L0_ffn_mu"], 8),
        "iw": pack_col(inp["L0_ffn_integration_weight"], 8),
        "WinT": np.ascontiguousarray(W_in.T),
        "WoutT": np.ascontiguousarray(W_out.T),
    }
    dirsets = {}
    for d in ("fwd", "rev"):
        dirsets[d] = {
            "WxT": np.ascontiguousarray(np.asarray(inp[f"L0_attn_Wx_{d}"], f32).T),
            "WdtT": np.ascontiguousarray(np.asarray(inp[f"L0_attn_Wdt_{d}"], f32).T),
            "convw": pack_cols(inp[f"L0_attn_conv_w_{d}"], DIC),
            "convb": pack_col(inp[f"L0_attn_conv_b_{d}"], DIC),
            "bdt": pack_col(inp[f"L0_attn_bdt_{d}"], DIC),
            "Alog": pack_cols(inp[f"L0_attn_A_log_{d}"], DIC),
            "Dp": pack_col(inp[f"L0_attn_D_{d}"], DIC),
        }
    in_maps = []
    for c in range(8):
        b_, h_ = c // 2, c % 2
        if h_ == 0:
            xs = x[b_, 0:OWN, :]
        else:
            xs = x[b_, 2 * OWN - 1:None if OWN == 0 else OWN - 1:-1, :] \
                if OWN > 1 else x[b_, OWN:2 * OWN, :][::-1]
            xs = x[b_, OWN:2 * OWN, :][::-1]
        m = dict(shared)
        m["xloc"] = np.ascontiguousarray(xs.T)
        sel = np.zeros((128, 2), f32)
        sel[:, 1 - h_] = 1.0
        m["selmask"] = sel
        p1d, p2d = ("fwd", "rev") if h_ == 0 else ("rev", "fwd")
        for p, d in (("p1", p1d), ("p2", p2d)):
            for kk, vv in dirsets[d].items():
                m[f"{kk}_{p}"] = vv
        in_maps.append(m)
    return in_maps


def kernel(**inputs):
    L = inputs["L0_attn_hidden_states"].shape[1]
    OWN = L // 2
    TF = 256 if OWN % 256 == 0 else 128
    TM = 256 if OWN % 256 == 0 else 128
    nc, _ = build_program(OWN, TF, TM)
    in_maps = _prep_core_inputs(inputs, OWN)
    res = run_bass_kernel_spmd(nc, in_maps, core_ids=list(range(8)))
    Bsz = inputs["L0_attn_hidden_states"].shape[0]
    out = np.empty((Bsz, L, DM), np.float32)
    for c in range(8):
        b_, h_ = c // 2, c % 2
        o = np.asarray(res.results[c]["out"])
        if h_ == 0:
            out[b_, 0:OWN, :] = o.T[::-1]
        else:
            out[b_, OWN:2 * OWN, :] = o.T
    return out
